# version 12
# baseline (speedup 1.0000x reference)
import math
from contextlib import ExitStack

import numpy as np
import concourse.bass as bass
import concourse.mybir as mybir
from concourse.bass_utils import run_bass_kernel_spmd

F32 = mybir.dt.float32
BF16 = mybir.dt.bfloat16
I32 = mybir.dt.int32
U16 = mybir.dt.uint16
AF = mybir.ActivationFunctionType
ALU = mybir.AluOpType

P = 128
TL = 1024
C = 4096
NCC = 32
H = 32
HD = 128
DFF = 11008
DFFP = 11264
NFC = 88
FSP = 1408
OSL = 512
KCNT = [12, 16]
NF4 = [-1.0, -0.6961928009986877, -0.5250730514526367, -0.39491748809814453,
       -0.28444138169288635, -0.18477343022823334, -0.09105003625154495, 0.0,
       0.07958029955625534, 0.16093020141124725, 0.2461123019456863,
       0.33791524171829224, 0.44070982933044434, 0.5626170039176941,
       0.7229568362236023, 1.0]
TWO_PI = 2.0 * math.pi
SM_SCALE = 1.0 / math.sqrt(HD)


class Ctr:
    TRACE = None
    CUR_ENG = [None]

    def __init__(self, h, name=""):
        self.h = h
        self.v = 0
        self.name = name

    def inc(self, n=1):
        self.v += n
        return self.v

    def adv(self, n=1):
        self.v += n
        if Ctr.TRACE is not None and Ctr.CUR_ENG[0] is not None:
            Ctr.TRACE.append((Ctr.CUR_ENG[0], "i", self.name, n))
        return n


def build_nc(phase_limit=99, audit=None):
    nc = bass.Bass("TRN2", target_bir_lowering=False, debug=False)

    def din(name, shape, dt):
        return nc.dram_tensor(name, shape, dt, kind="ExternalInput").ap()

    x_in = din("x_in", [TL, C], F32)
    pos_in = din("pos_in", [1, TL], F32)
    wT = {"k": din("wkT", [C, C], F32), "v": din("wvT", [C, C], F32),
          "q": din("wqT", [C, C], F32), "o": din("woT", [C, C], F32)}
    gcodesT = din("gcodesT", [C, FSP], I32)
    ucodesT = din("ucodesT", [C, FSP], I32)
    dcodesT = din("dcodesT", [DFFP, OSL], I32)
    gamaxR = din("gamaxR", [C, FSP], F32)
    uamaxR = din("uamaxR", [C, FSP], F32)
    damaxR = din("damaxR", [DFFP, OSL], F32)
    gaT_in = din("gaT_in", [16, C], F32)
    uaT_in = din("uaT_in", [16, C], F32)
    daT_in = din("daT_in", [16, DFFP], F32)
    gb_in = din("gb_in", [16, FSP], F32)
    ub_in = din("ub_in", [16, FSP], F32)
    db_in = din("db_in", [16, OSL], F32)
    out_ext = nc.dram_tensor("out", [TL, C], F32, kind="ExternalOutput").ap()

    kcon = nc.dram_tensor("kcon", [H * HD, TL], BF16)
    vcon = nc.dram_tensor("vcon", [H * TL, HD], BF16)
    kg = nc.dram_tensor("kg", [2 * H * HD, TL], BF16)
    vg = nc.dram_tensor("vg", [2 * H * TL, HD], BF16)
    gcon = nc.dram_tensor("gcon", [NCC * 11 * P, P], BF16)
    ucon = nc.dram_tensor("ucon", [NCC * 11 * P, P], BF16)
    dcon = nc.dram_tensor("dcon", [DFFP, OSL], BF16)
    gg = nc.dram_tensor("gg", [8 * NCC * 11 * P, P], BF16)
    ug = nc.dram_tensor("ug", [8 * NCC * 11 * P, P], BF16)
    dg = nc.dram_tensor("dg", [8 * DFFP, OSL], BF16)
    xnf_dram = nc.dram_tensor("xnf_dram", [C, TL], BF16)
    attn_dram = nc.dram_tensor("attn_dram", [TL, C], F32)
    x3_dram = nc.dram_tensor("x3_dram", [DFFP, TL], BF16)
    lorab_dram = nc.dram_tensor("lorab_dram", [16, 2 * C + DFFP], BF16)

    stk = ExitStack()
    sb = lambda n, s, d: stk.enter_context(nc.sbuf_tensor(n, s, d))
    pst = lambda n, s, d: stk.enter_context(nc.psum_tensor(n, s, d))

    BIG1 = sb("BIG1", [P, 32768], BF16)
    BIG2 = sb("BIG2", [P, 32768], BF16)
    U = sb("U", [P, 32768], BF16)
    posq = sb("posq", [P, TL], F32)
    sincos = sb("sincos", [P, 2 * TL], BF16)
    ident = sb("ident", [P, P], BF16)
    posk = sb("posk", [P, 16], F32)
    theta = sb("theta", [P, 1], F32)
    ones_col = sb("ones_col", [P, 1], F32)
    ones_row = sb("ones_row", [1, P], F32)
    consts = sb("consts", [P, 4], F32)
    stats = sb("stats", [P, 8], F32)
    gb_sl = sb("gb_sl", [16, FSP], BF16)
    ub_sl = sb("ub_sl", [16, FSP], BF16)
    db_sl = sb("db_sl", [16, OSL], BF16)

    def uf32(o, n):
        return U[:, o:o + n].bitcast(F32)

    ir = U[:, 14336:14592].bitcast(I32)
    ic = U[:, 14592:14848].bitcast(I32)
    possb = U[0:1, 14848:16896].bitcast(F32)
    zacc = uf32(22528, 4096)
    rz = U[0:1, 26624:27648].bitcast(F32)
    rzrep = uf32(27648, 1024)
    op_stage = [sincos[:, i * 1024:(i + 1) * 1024].bitcast(F32) for i in range(2)]
    xtok = uf32(0, 8192)
    xn_tok = [U[:, 8192:12288], U[:, 12288:16384]]
    lpre = uf32(8192, 4096)
    lpre_b = U[:, 12288:14336]
    dq_codes = [U[:, 16384 + i * 1024:16384 + (i + 1) * 1024].bitcast(I32) for i in range(2)]
    dq_cu16 = U[:, 18432:18944].bitcast(U16)
    dq_m0 = U[:, 18944:19456].bitcast(U16)
    dq_m0b = U[:, 19456:19968]
    dq_m1 = U[:, 19968:20480].bitcast(U16)
    dq_m2 = U[:, 20480:20992].bitcast(U16)
    dq_m3 = U[:, 20992:21504].bitcast(U16)
    dq_b8 = U[:, 21504:25600]
    dq_tb1 = U[:, 18432:18944]
    dq_tb2 = U[:, 18944:19456]
    dq_t4 = U[:, 25600:27648]
    dq_u2 = U[:, 27648:28672]
    dq_v1 = U[:, 28672:29184]
    dq_arep = [uf32(29184 + i * 1024, 1024) for i in range(2)]
    dq_w = [U[:, 31232 + i * 512:31232 + (i + 1) * 512] for i in range(2)]
    dq_lt = [U[:, 32256 + i * 128:32256 + (i + 1) * 128] for i in range(2)]
    kev = [U[:, 8192 + i * 512:8192 + (i + 1) * 512] for i in range(2)]
    vev = [U[:, 9216 + i * 512:9216 + (i + 1) * 512] for i in range(2)]
    vtr = [U[:, 10240 + i * 512:10240 + (i + 1) * 512] for i in range(2)]
    rt1 = uf32(11264, 1024)
    rt2 = uf32(13312, 1024)
    strip_qo = [U[:, i * 4096:(i + 1) * 4096] for i in range(2)]
    at_bins = U[:, 0:12288]
    at_kh = [U[:, 12288 + i * 2048:12288 + (i + 1) * 2048] for i in range(2)]
    at_vh = [U[:, 16384 + i * 2048:16384 + (i + 1) * 2048] for i in range(2)]
    at_pex = [U[:, 20480 + i * 512:20480 + (i + 1) * 512] for i in range(2)]
    at_pm = [U[:, 21504 + i * 512:21504 + (i + 1) * 512] for i in range(2)]
    wo_buf = [U[:, i * 16384:(i + 1) * 16384] for i in range(2)]
    fg_w = [U[:, i * 4096:(i + 1) * 4096] for i in range(4)]
    fy2 = [U[:, 16384 + i * 512:16384 + (i + 1) * 512] for i in range(2)]
    fx3 = [U[:, 17408 + i * 512:17408 + (i + 1) * 512] for i in range(2)]
    fd_w = [U[:, i * 4096:(i + 1) * 4096] for i in range(2)]
    fd_x3 = [U[:, 8192 + i * 512:8192 + (i + 1) * 512] for i in range(2)]
    fo_at = [uf32(9216 + i * 1024, 1024) for i in range(2)]
    fo_x = [uf32(11264 + i * 1024, 1024) for i in range(2)]
    fo_a = [uf32(13312 + i * 1024, 1024) for i in range(2)]
    fo_b = [uf32(18432 + i * 1024, 1024) for i in range(2)]

    qT = BIG2
    xn_f = BIG1
    outT = BIG1
    xn_f2 = BIG2
    strip_kv = [BIG2[:, i * 4096:(i + 1) * 4096] for i in range(2)]
    rope_pt = BIG2[:, 24576:26624].bitcast(F32)
    rope_kf = BIG2[:, 26624:28672].bitcast(F32)
    rope_ki = BIG2[:, 28672:30720].bitcast(I32)
    rope_ind = BIG2[:, 28672:30720].bitcast(F32)
    rope_rr = BIG2[:, 30720:32768].bitcast(F32)

    ps_f = [pst(f"ps{i}", [P, 512], F32) for i in range(5)]
    ps_z = pst("ps_z", [1, 512], F32)
    ps_bb = [pst("ps_bA", [P, 1024], BF16), pst("ps_bB", [P, 1024], BF16)]

    names = ["s_pos", "s_x", "s_lorain", "s_lorabf",
             "s_cod0", "s_cod1", "s_arep0", "s_arep1", "s_lt0", "s_lt1",
             "s_kv0", "s_kv1", "s_fg0", "s_fg1", "s_fd0", "s_fd1",
             "s_xr0", "s_xr1", "s_ao0", "s_ao1", "s_xnf", "s_xnfl",
             "g_init", "g_st0", "g_st1", "g_wo0", "g_wo1", "g_cckv", "g_ccw",
             "s_dst0", "s_dst1", "s_kst0", "s_kst1", "s_vst0", "s_vst1",
             "s_opst0", "s_opst1", "s_x3st0", "s_x3st1", "s_outst0", "s_outst1",
             "q_st", "q_out", "v_self", "s_self",
             "v_ln", "s_lns", "s_sqrt", "t_tr", "v_tre",
             "v_dqm", "s_dqb", "t_dql", "v_dqw",
             "t_posb", "v_posb", "v_rtab", "s_sin", "v_pre",
             "t_pj", "v_pje", "t_vtr", "v_vte",
             "t_s", "s_px", "v_pm", "t_pv", "t_zc", "v_rz", "t_rzb", "v_oev",
             "v_bin", "t_op", "v_ope",
             "t_y", "s_y2c", "v_x3", "t_d", "v_de", "t_kvf", "t_strf",
             "v_ssync", "s_ssync"]
    SM = {n: Ctr(stk.enter_context(nc.semaphore(n)), n) for n in names}

    prog = []
    cur_phase = [0]
    Ctr.TRACE = audit

    class _MockInst:
        def then_inc(self, h, n):
            return self

    class _MockEng:
        def __getattr__(self, name):
            def f(*a, **k):
                return _MockInst()
            return f

    _MOCK = _MockEng()

    def set_phase(n):
        cur_phase[0] = n

    def on(eng, fn):
        Ctr.CUR_ENG[0] = eng
        fn(_MOCK)
        Ctr.CUR_ENG[0] = None
        prog.append((eng, fn, cur_phase[0]))

    def wait(eng, ctr, val):
        if val > 0:
            if Ctr.TRACE is not None:
                Ctr.TRACE.append((eng, "w", ctr.name, val))
            prog.append((eng, lambda e, h=ctr.h, v=val: e.wait_ge(h, v), cur_phase[0]))

    def g0(e):
        e.iota(ir, [[1, P]], base=0, channel_multiplier=0)
        e.iota(ic, [[0, P]], base=0, channel_multiplier=1)
        e.iota(posk[:], [[P, 16]], base=0, channel_multiplier=1,
               allow_small_or_imprecise_dtypes=True)
        e.iota(theta[0:64, :], [[0, 1]], base=0, channel_multiplier=1,
               allow_small_or_imprecise_dtypes=True)
        e.iota(theta[64:128, :], [[0, 1]], base=0, channel_multiplier=1,
               allow_small_or_imprecise_dtypes=True)
        e.memset(ones_col[:], 1.0)
        e.memset(ones_row[:], 1.0)
        e.memset(consts[:, 0:1], 1e-5)
        e.memset(consts[:, 1:2], -math.pi).then_inc(SM["g_init"].h, SM["g_init"].adv())
    on("gpsimd", g0)
    wait("vector", SM["g_init"], 1)
    on("vector", lambda e: e.tensor_tensor(ident[:], ir, ic, ALU.is_equal))
    wait("scalar", SM["g_init"], 1)
    on("scalar", lambda e: e.activation(theta[:], theta[:], AF.Exp,
                                        scale=-math.log(10000.0) / 64.0)
       .then_inc(SM["s_self"].h, SM["s_self"].adv()))
    on("sync", lambda e: e.dma_start(out=possb, in_=pos_in)
       .then_inc(SM["s_pos"].h, SM["s_pos"].adv(16)))
    wait("tensor", SM["s_pos"], 16)
    wait("tensor", SM["g_init"], 1)
    on("tensor", lambda e: e.matmul(ps_f[4][:], ones_row[:], possb[0:1, 0:512],
                                    start=True, stop=True)
       .then_inc(SM["t_posb"].h, SM["t_posb"].adv()))
    wait("vector", SM["t_posb"], 1)
    on("vector", lambda e: e.tensor_copy(posq[:, 0:512], ps_f[4][:])
       .then_inc(SM["v_posb"].h, SM["v_posb"].adv()))
    wait("tensor", SM["v_posb"], 1)
    on("tensor", lambda e: e.matmul(ps_f[4][:], ones_row[:], possb[0:1, 512:1024],
                                    start=True, stop=True)
       .then_inc(SM["t_posb"].h, SM["t_posb"].adv()))
    wait("vector", SM["t_posb"], 2)
    on("vector", lambda e: e.tensor_copy(posq[:, 512:1024], ps_f[4][:])
       .then_inc(SM["v_posb"].h, SM["v_posb"].adv()))
    wait("vector", SM["s_self"], 1)
    wait("vector", SM["v_posb"], 2)

    def vsin(e):
        e.tensor_scalar(rope_pt[:], posq[:], theta[:, 0:1], None, ALU.mult)
        e.tensor_scalar(rope_kf[:], rope_pt[:], 1.0 / TWO_PI, None, ALU.mult)
        e.tensor_copy(rope_ki[:], rope_kf[:])
        e.tensor_copy(rope_kf[:], rope_ki[:])
        e.scalar_tensor_tensor(rope_rr[:], rope_kf[:], -TWO_PI, rope_pt[:],
                               ALU.mult, ALU.add)
        e.tensor_scalar(rope_ind[:], rope_rr[:], math.pi, None, ALU.is_gt)
        e.scalar_tensor_tensor(rope_kf[:], rope_ind[:], -TWO_PI, rope_rr[:],
                               ALU.mult, ALU.add)
        e.tensor_scalar(rope_ind[:], rope_kf[:], -math.pi, None, ALU.is_lt)
        e.scalar_tensor_tensor(rope_rr[:], rope_ind[:], TWO_PI, rope_kf[:],
                               ALU.mult, ALU.add).then_inc(SM["v_rtab"].h, SM["v_rtab"].adv())
    on("vector", vsin)
    wait("scalar", SM["v_rtab"], 1)
    on("scalar", lambda e: e.activation(sincos[:, 0:TL], rope_rr[:], AF.Sin)
       .then_inc(SM["s_sin"].h, SM["s_sin"].adv()))
    wait("vector", SM["s_sin"], 1)

    def vcos(e):
        e.tensor_scalar(rope_kf[:], rope_pt[:], 1.0, math.pi / 2.0, ALU.mult, ALU.add)
        e.tensor_scalar(rope_rr[:], rope_kf[:], 1.0 / TWO_PI, None, ALU.mult)
        e.tensor_copy(rope_ki[:], rope_rr[:])
        e.tensor_copy(rope_pt[:], rope_ki[:])
        e.scalar_tensor_tensor(rope_rr[:], rope_pt[:], -TWO_PI, rope_kf[:],
                               ALU.mult, ALU.add)
        e.tensor_scalar(rope_ind[:], rope_rr[:], math.pi, None, ALU.is_gt)
        e.scalar_tensor_tensor(rope_pt[:], rope_ind[:], -TWO_PI, rope_rr[:],
                               ALU.mult, ALU.add)
        e.tensor_scalar(rope_ind[:], rope_pt[:], -math.pi, None, ALU.is_lt)
        e.scalar_tensor_tensor(rope_rr[:], rope_ind[:], TWO_PI, rope_pt[:],
                               ALU.mult, ALU.add).then_inc(SM["v_rtab"].h, SM["v_rtab"].adv())
    on("vector", vcos)
    wait("scalar", SM["v_rtab"], 2)
    on("scalar", lambda e: e.activation(sincos[:, TL:2 * TL], rope_rr[:], AF.Sin)
       .then_inc(SM["s_sin"].h, SM["s_sin"].adv()))

    for (src, dst_off, width) in [(gaT_in, 0, C), (uaT_in, C, C), (daT_in, 2 * C, DFFP)]:
        for c0 in range(0, width, 2048):
            w = min(2048, width - c0)
            on("sync", lambda e, src=src, c0=c0, w=w:
               e.dma_start(out=lpre[0:16, 0:w], in_=src[:, c0:c0 + w])
               .then_inc(SM["s_lorain"].h, SM["s_lorain"].adv(16)))
            wait("vector", SM["s_lorain"], SM["s_lorain"].v)
            on("vector", lambda e, w=w: e.tensor_copy(lpre_b[0:16, 0:w], lpre[0:16, 0:w])
               .then_inc(SM["v_pre"].h, SM["v_pre"].adv()))
            wait("scalar", SM["v_pre"], SM["v_pre"].v)
            on("scalar", lambda e, dst_off=dst_off, c0=c0, w=w:
               e.dma_start(out=lorab_dram.ap()[:, dst_off + c0:dst_off + c0 + w],
                           in_=lpre_b[0:16, 0:w])
               .then_inc(SM["s_lorabf"].h, SM["s_lorabf"].adv(16)))
            wait("sync", SM["s_lorabf"], SM["s_lorabf"].v)
    for (src, dstt, width) in [(gb_in, gb_sl, FSP), (ub_in, ub_sl, FSP), (db_in, db_sl, OSL)]:
        on("sync", lambda e, src=src, width=width:
           e.dma_start(out=lpre[0:16, 0:width], in_=src[:, 0:width])
           .then_inc(SM["s_lorain"].h, SM["s_lorain"].adv(16)))
        wait("vector", SM["s_lorain"], SM["s_lorain"].v)
        on("vector", lambda e, dstt=dstt, width=width:
           e.tensor_copy(dstt[0:16, 0:width], lpre[0:16, 0:width])
           .then_inc(SM["v_pre"].h, SM["v_pre"].adv()))
        wait("sync", SM["v_pre"], SM["v_pre"].v)
    V_PRE_ALL = SM["v_pre"].v
    LORABF_ALL = SM["s_lorabf"].v

    set_phase(1)
    for i in range(8):
        if i > 0:
            wait("sync", SM["v_ln"], i)
        on("sync", lambda e, i=i: e.dma_start(out=xtok[:], in_=x_in[i * P:(i + 1) * P, :])
           .then_inc(SM["s_x"].h, SM["s_x"].adv(16)))
        wait("scalar", SM["s_x"], SM["s_x"].v)
        if i >= 2:
            wait("scalar", SM["t_tr"], 8 * (i - 1))
        on("scalar", lambda e, i=i: e.activation(xn_tok[i % 2][:], xtok[:], AF.Copy,
                                                 accum_out=stats[:, 0:1]))
        on("scalar", lambda e, i=i: e.activation(xn_tok[i % 2][:], xtok[:], AF.Square,
                                                 accum_out=stats[:, 1:2])
           .then_inc(SM["s_lns"].h, SM["s_lns"].adv()))
        wait("vector", SM["s_lns"], SM["s_lns"].v)
        on("vector", lambda e: e.tensor_scalar(stats[:, 2:3], stats[:, 0:1], 1.0 / C, None, ALU.mult))
        on("vector", lambda e: e.tensor_scalar(stats[:, 3:4], stats[:, 1:2], 1.0 / C, None, ALU.mult)
           .then_inc(SM["v_self"].h, SM["v_self"].adv()))
        wait("vector", SM["v_self"], SM["v_self"].v)
        on("vector", lambda e: e.tensor_tensor(stats[:, 4:5], stats[:, 2:3], stats[:, 2:3], ALU.mult)
           .then_inc(SM["v_self"].h, SM["v_self"].adv()))
        wait("vector", SM["v_self"], SM["v_self"].v)
        on("vector", lambda e: e.tensor_tensor(stats[:, 5:6], stats[:, 3:4], stats[:, 4:5], ALU.subtract)
           .then_inc(SM["v_self"].h, SM["v_self"].adv()))
        wait("scalar", SM["v_self"], SM["v_self"].v)
        on("scalar", lambda e: e.activation(stats[:, 6:7], stats[:, 5:6], AF.Sqrt,
                                            bias=consts[:, 0:1])
           .then_inc(SM["s_sqrt"].h, SM["s_sqrt"].adv()))
        wait("vector", SM["s_sqrt"], SM["s_sqrt"].v)
        on("vector", lambda e: e.reciprocal(stats[:, 7:8], stats[:, 6:7])
           .then_inc(SM["v_self"].h, SM["v_self"].adv()))
        wait("vector", SM["v_self"], SM["v_self"].v)
        if i >= 1:
            wait("vector", SM["t_tr"], SM["t_tr"].v - 8 if SM["t_tr"].v >= 8 else 0)
        on("vector", lambda e, i=i: e.tensor_scalar(xn_tok[i % 2][:], xtok[:],
                                                    stats[:, 2:3], stats[:, 7:8],
                                                    ALU.subtract, ALU.mult)
           .then_inc(SM["v_ln"].h, SM["v_ln"].adv()))
        wait("tensor", SM["v_ln"], SM["v_ln"].v)
        for b in range(8):
            half = b % 2
            if SM["t_tr"].v >= 2:
                wait("tensor", SM["v_tre"], SM["t_tr"].v - 1)
            def ttr(e, i=i, b=b, half=half):
                inst = None
                for q in range(4):
                    cc = b * 4 + q
                    inst = e.transpose(
                        ps_bb[half][:, q * 128:(q + 1) * 128],
                        xn_tok[i % 2][:, cc * 128:(cc + 1) * 128], ident[:])
                inst.then_inc(SM["t_tr"].h, SM["t_tr"].adv())
            on("tensor", ttr)
            wait("vector", SM["t_tr"], SM["t_tr"].v)
            def vtre(e, i=i, b=b, half=half):
                inst = None
                for q in range(4):
                    cc = b * 4 + q
                    inst = e.tensor_copy(
                        xn_f[:, cc * 1024 + i * 128:cc * 1024 + (i + 1) * 128],
                        ps_bb[half][:, q * 128:(q + 1) * 128])
                inst.then_inc(SM["v_tre"].h, SM["v_tre"].adv())
            on("vector", vtre)
    V_TRE_ALL = SM["v_tre"].v

    set_phase(2)
    dq_tiles = [("d", fc, 0, 512) for fc in range(NFC)]
    for wname in ("g", "u"):
        for cc in range(NCC):
            for (seg, W) in ((0, 512), (1, 512), (2, 384)):
                dq_tiles.append((wname, cc, seg, W))
    NDQ = len(dq_tiles)

    wait("tensor", SM["v_posb"], 1)
    dq_store_caps = []
    for ti, (wn, idx, seg, W) in enumerate(dq_tiles):
        par = ti % 2
        if wn == "d":
            codes_src = dcodesT[idx * P:(idx + 1) * P, 0:W]
            arep_src = damaxR[idx * P:(idx + 1) * P, 0:W]
            lt_off = 2 * C + idx * P
            lrhs = db_sl[0:16, 0:W]
        else:
            src_c = gcodesT if wn == "g" else ucodesT
            src_a = gamaxR if wn == "g" else uamaxR
            codes_src = src_c[idx * P:(idx + 1) * P, seg * 512:seg * 512 + W]
            arep_src = src_a[idx * P:(idx + 1) * P, seg * 512:seg * 512 + W]
            lt_off = (0 if wn == "g" else C) + idx * P
            lrhs = (gb_sl if wn == "g" else ub_sl)[0:16, seg * 512:seg * 512 + W]
        if ti >= 2:
            wait("sync", SM["v_dqw"], ti - 1)
        on("sync", lambda e, par=par, codes_src=codes_src, W=W:
           e.dma_start(out=dq_codes[par][:, 0:W], in_=codes_src)
           .then_inc(SM[f"s_cod{par}"].h, SM[f"s_cod{par}"].adv(16)))
        cod_sem, cod_cap = SM[f"s_cod{par}"], SM[f"s_cod{par}"].v
        on("sync", lambda e, par=par, arep_src=arep_src, W=W:
           e.dma_start(out=dq_arep[par][:, 0:W], in_=arep_src)
           .then_inc(SM[f"s_arep{par}"].h, SM[f"s_arep{par}"].adv(16)))
        arep_sem, arep_cap = SM[f"s_arep{par}"], SM[f"s_arep{par}"].v
        if ti == 0:
            wait("sync", SM["s_lorabf"], LORABF_ALL)
        if ti >= 2:
            wait("sync", SM["t_dql"], ti - 1)
        on("sync", lambda e, par=par, lt_off=lt_off:
           e.dma_start(out=dq_lt[par][0:16, :], in_=lorab_dram.ap()[:, lt_off:lt_off + P])
           .then_inc(SM[f"s_lt{par}"].h, SM[f"s_lt{par}"].adv(16)))
        lt_sem, lt_cap = SM[f"s_lt{par}"], SM[f"s_lt{par}"].v
        wait("vector", cod_sem, cod_cap)
        def vmask(e, par=par, W=W):
            e.tensor_copy(dq_cu16[:, 0:W], dq_codes[par][:, 0:W])
            e.tensor_scalar(dq_m0[:, 0:W], dq_cu16[:, 0:W], 1, None, ALU.bitwise_and)
            e.tensor_copy(dq_m0b[:, 0:W], dq_m0[:, 0:W])
            e.tensor_scalar(dq_m1[:, 0:W], dq_cu16[:, 0:W], 1, 1,
                            ALU.logical_shift_right, ALU.bitwise_and)
            e.tensor_scalar(dq_m2[:, 0:W], dq_cu16[:, 0:W], 2, 1,
                            ALU.logical_shift_right, ALU.bitwise_and)
            e.tensor_scalar(dq_m3[:, 0:W], dq_cu16[:, 0:W], 3, 1,
                            ALU.logical_shift_right, ALU.bitwise_and) \
                .then_inc(SM["v_dqm"].h, SM["v_dqm"].adv())
        on("vector", vmask)
        wait("scalar", SM["v_dqm"], SM["v_dqm"].v)
        if ti >= 1:
            wait("scalar", SM["v_dqw"], ti)
        def sblend(e, W=W):
            inst = None
            for k in range(8):
                lo = NF4[2 * k]
                hi = NF4[2 * k + 1]
                inst = e.activation(dq_b8[:, k * 512:k * 512 + W], dq_m0b[:, 0:W],
                                    AF.Copy, bias=float(lo), scale=float(hi - lo))
            inst.then_inc(SM["s_dqb"].h, SM["s_dqb"].adv())
        on("scalar", sblend)
        if ti == 0:
            wait("tensor", SM["v_pre"], V_PRE_ALL)
        wait("tensor", lt_sem, lt_cap)
        if ti >= 1:
            wait("tensor", SM["v_dqw"], ti)
        on("tensor", lambda e, par=par, W=W, lrhs=lrhs:
           e.matmul(ps_f[4][:, 0:W], dq_lt[par][0:16, :], lrhs,
                    start=True, stop=True).then_inc(SM["t_dql"].h, SM["t_dql"].adv()))
        wait("vector", SM["s_dqb"], SM["s_dqb"].v)
        wait("vector", arep_sem, arep_cap)
        wait("vector", SM["t_dql"], SM["t_dql"].v)
        if ti >= 2:
            wait("vector", *dq_store_caps[ti - 2])
        def vtree(e, par=par, W=W):
            def blend(out, m, hi, lo):
                e.tensor_tensor(dq_tb1[:, 0:W], hi, lo, ALU.subtract)
                e.tensor_tensor(dq_tb2[:, 0:W], dq_tb1[:, 0:W], m, ALU.mult)
                e.tensor_tensor(out, dq_tb2[:, 0:W], lo, ALU.add)
            for j in range(4):
                blend(dq_t4[:, j * 512:j * 512 + W], dq_m1[:, 0:W],
                      dq_b8[:, (2 * j + 1) * 512:(2 * j + 1) * 512 + W],
                      dq_b8[:, (2 * j) * 512:(2 * j) * 512 + W])
            for k in range(2):
                blend(dq_u2[:, k * 512:k * 512 + W], dq_m2[:, 0:W],
                      dq_t4[:, (2 * k + 1) * 512:(2 * k + 1) * 512 + W],
                      dq_t4[:, (2 * k) * 512:(2 * k) * 512 + W])
            blend(dq_v1[:, 0:W], dq_m3[:, 0:W],
                  dq_u2[:, 512:512 + W], dq_u2[:, 0:W])
            e.tensor_tensor(dq_t4[:, 0:W], dq_v1[:, 0:W], dq_arep[par][:, 0:W], ALU.mult)
            e.tensor_tensor(dq_w[par][:, 0:W], dq_t4[:, 0:W],
                            ps_f[4][:, 0:W], ALU.add) \
                .then_inc(SM["v_dqw"].h, SM["v_dqw"].adv())
        on("vector", vtree)
        wait("scalar", SM["v_dqw"], SM["v_dqw"].v)
        def qstore(e, wn=wn, idx=idx, seg=seg, W=W, par=par):
            if wn == "d":
                e.dma_start(out=dcon.ap()[idx * P:(idx + 1) * P, :],
                            in_=dq_w[par][:, 0:W]).then_inc(SM[f"s_dst{par}"].h, SM[f"s_dst{par}"].adv(16))
            else:
                con = gcon if wn == "g" else ucon
                nb = W // 128
                fb0 = seg * 4
                e.dma_start(
                    out=con.ap()[idx * FSP + fb0 * P:idx * FSP + (fb0 + nb) * P, :]
                    .rearrange("(b p) f -> p b f", p=P),
                    in_=dq_w[par][:, 0:W].rearrange("p (b f) -> p b f", f=128)
                ).then_inc(SM[f"s_dst{par}"].h, SM[f"s_dst{par}"].adv(16))
        on("scalar", qstore)
        dq_store_caps.append((SM[f"s_dst{par}"], SM[f"s_dst{par}"].v))
    V_DQW_ALL = SM["v_dqw"].v
    DQ_ST_ALL = [(SM["s_dst0"], SM["s_dst0"].v), (SM["s_dst1"], SM["s_dst1"].v)]

    set_phase(3)
    kv_store_caps = {}
    strip_free = {"kv": [], "qo": []}

    def emit_proj(wname, strip_bufs, kind):
        sfree = strip_free["qo" if kind == "q" else "kv"]
        for dc in range(NCC):
            par = dc % 2
            if len(sfree) >= 2:
                wait("gpsimd", SM["v_pje"], sfree[-2])
            def gload(e, wname=wname, dc=dc, par=par, strip_bufs=strip_bufs):
                e.dma_start(
                    out=strip_bufs[par].rearrange("p (a d) -> p a d", d=128),
                    in_=wT[wname][:, dc * 128:(dc + 1) * 128]
                    .rearrange("(a p) d -> p a d", p=P)
                ).then_inc(SM[f"g_st{par}"].h, SM[f"g_st{par}"].adv(16))
            on("gpsimd", gload)
            wait("tensor", SM[f"g_st{par}"], SM[f"g_st{par}"].v)
            if kind == "k" and dc == 0:
                wait("tensor", SM["v_tre"], V_TRE_ALL)
            for th in range(2):
                bank = (dc % 2) * 2 + th
                grp = SM["t_pj"].v + 1
                if grp > 4:
                    wait("tensor", SM["v_pje"], grp - 4)
                def tmm(e, dc=dc, th=th, bank=bank, par=par, strip_bufs=strip_bufs):
                    inst = None
                    for cc in range(NCC):
                        inst = e.matmul(
                            ps_f[bank][:],
                            strip_bufs[par][:, cc * 128:(cc + 1) * 128],
                            xn_f[:, cc * 1024 + th * 512:cc * 1024 + (th + 1) * 512],
                            start=(cc == 0), stop=(cc == NCC - 1))
                    inst.then_inc(SM["t_pj"].h, SM["t_pj"].adv())
                on("tensor", tmm)
                wait("vector", SM["t_pj"], SM["t_pj"].v)
                if kind in ("k", "q"):
                    j = dc * 2 + th
                    if kind == "k" and j >= 2:
                        wait("vector", *kv_store_caps[("k", j - 2)])
                    def vrope_evac(e, dc=dc, th=th, bank=bank, kind=kind):
                        cs = sincos[:, TL + th * 512:TL + (th + 1) * 512]
                        sn = sincos[:, th * 512:(th + 1) * 512]
                        dst = (kev[(dc * 2 + th) % 2][:] if kind == "k"
                               else qT[:, dc * 1024 + th * 512:dc * 1024 + (th + 1) * 512])
                        p = ps_f[bank]
                        e.tensor_tensor(rt1[0:64, :], p[0:64, :], cs[0:64, :], ALU.mult)
                        e.scalar_tensor_tensor(rt2[0:64, :], p[64:128, :], -1.0,
                                               sn[0:64, :], ALU.mult, ALU.mult)
                        e.tensor_tensor(dst[0:64, :], rt1[0:64, :], rt2[0:64, :], ALU.add)
                        e.tensor_tensor(rt1[64:128, :], p[64:128, :], cs[64:128, :], ALU.mult)
                        e.tensor_tensor(rt2[64:128, :], p[0:64, :], sn[64:128, :], ALU.mult)
                        e.tensor_tensor(dst[64:128, :], rt1[64:128, :],
                                        rt2[64:128, :], ALU.add) \
                            .then_inc(SM["v_pje"].h, SM["v_pje"].adv())
                    on("vector", vrope_evac)
                    if th == 1:
                        sfree.append(SM["v_pje"].v)
                    if kind == "k":
                        wait("scalar", SM["v_pje"], SM["v_pje"].v)
                        jp = (dc * 2 + th) % 2
                        on("scalar", lambda e, dc=dc, th=th, jp=jp:
                           e.dma_start(out=kcon.ap()[dc * 128:(dc + 1) * 128,
                                                     th * 512:(th + 1) * 512],
                                       in_=kev[(dc * 2 + th) % 2][:])
                           .then_inc(SM[f"s_kst{jp}"].h, SM[f"s_kst{jp}"].adv(16)))
                        kv_store_caps[("k", dc * 2 + th)] = (SM[f"s_kst{jp}"], SM[f"s_kst{jp}"].v)
                else:
                    def vv_evac(e, th=th, bank=bank):
                        e.tensor_copy(vev[th][:], ps_f[bank][:]) \
                            .then_inc(SM["v_pje"].h, SM["v_pje"].adv())
                    on("vector", vv_evac)
                    if th == 1:
                        sfree.append(SM["v_pje"].v)
                    wait("tensor", SM["v_pje"], SM["v_pje"].v)
                    if SM["t_vtr"].v >= 2:
                        wait("tensor", SM["v_vte"], SM["t_vtr"].v - 1)
                    def tvt(e, th=th):
                        inst = None
                        for q in range(4):
                            inst = e.transpose(
                                ps_bb[th][:, q * 128:(q + 1) * 128],
                                vev[th][:, q * 128:(q + 1) * 128], ident[:])
                        inst.then_inc(SM["t_vtr"].h, SM["t_vtr"].adv())
                    on("tensor", tvt)
                    wait("vector", SM["t_vtr"], SM["t_vtr"].v)
                    j = dc * 2 + th
                    if j >= 2:
                        wait("vector", *kv_store_caps[("v", j - 2)])
                    on("vector", lambda e, th=th:
                       e.tensor_copy(vtr[th][:], ps_bb[th][:, 0:512])
                       .then_inc(SM["v_vte"].h, SM["v_vte"].adv()))
                    wait("scalar", SM["v_vte"], SM["v_vte"].v)
                    jp = j % 2
                    on("scalar", lambda e, dc=dc, th=th, jp=jp:
                       e.dma_start(out=vcon.ap()[dc * 1024 + th * 512:
                                                 dc * 1024 + (th + 1) * 512, :]
                                   .rearrange("(q p) d -> p q d", p=P),
                                   in_=vtr[th][:].rearrange("p (q d) -> p q d", d=128))
                       .then_inc(SM[f"s_vst{jp}"].h, SM[f"s_vst{jp}"].adv(16)))
                    kv_store_caps[("v", j)] = (SM[f"s_vst{jp}"], SM[f"s_vst{jp}"].v)

    set_phase(31)
    emit_proj("k", strip_kv, "k")
    set_phase(32)
    emit_proj("v", strip_kv, "v")
    set_phase(33)
    for s in ["s_kst0", "s_kst1", "s_vst0", "s_vst1"]:
        wait("gpsimd", SM[s], SM[s].v)
    on("gpsimd", lambda e: e.collective_compute(
        "AllGather", ALU.bypass, replica_groups=[[0, 1], [2, 3], [4, 5], [6, 7]],
        ins=[kcon.ap()], outs=[kg.ap()]).then_inc(SM["g_cckv"].h, SM["g_cckv"].adv()))
    on("gpsimd", lambda e: e.collective_compute(
        "AllGather", ALU.bypass, replica_groups=[[0, 1], [2, 3], [4, 5], [6, 7]],
        ins=[vcon.ap()], outs=[vg.ap()]).then_inc(SM["g_cckv"].h, SM["g_cckv"].adv()))
    set_phase(34)
    wait("gpsimd", SM["g_cckv"], 2)
    emit_proj("q", strip_qo, "q")
    T_PJ_ALL = SM["t_pj"].v
    V_PJE_ALL = SM["v_pje"].v
    set_phase(35)
    for s, v in DQ_ST_ALL:
        wait("gpsimd", s, v)
    on("gpsimd", lambda e: e.collective_compute(
        "AllGather", ALU.bypass, replica_groups=[list(range(8))],
        ins=[gcon.ap()], outs=[gg.ap()]).then_inc(SM["g_ccw"].h, SM["g_ccw"].adv()))
    on("gpsimd", lambda e: e.collective_compute(
        "AllGather", ALU.bypass, replica_groups=[list(range(8))],
        ins=[ucon.ap()], outs=[ug.ap()]).then_inc(SM["g_ccw"].h, SM["g_ccw"].adv()))
    on("gpsimd", lambda e: e.collective_compute(
        "AllGather", ALU.bypass, replica_groups=[list(range(8))],
        ins=[dcon.ap()], outs=[dg.ap()]).then_inc(SM["g_ccw"].h, SM["g_ccw"].adv()))

    set_phase(36)
    wait("scalar", SM["v_pje"], V_PJE_ALL)
    on("scalar", lambda e: e.dma_start(out=xnf_dram.ap().rearrange("(a p) t -> p a t", p=P),
                                       in_=xn_f[:].rearrange("p (a t) -> p a t", t=TL))
       .then_inc(SM["s_xnf"].h, SM["s_xnf"].adv(16)))

    set_phase(40)
    wait("vector", SM["t_pj"], T_PJ_ALL)
    wait("vector", SM["v_posb"], 2)
    BIN_IDX = {}
    bi = 0
    for J in range(2):
        for kk in range(4 * J, KCNT[J]):
            BIN_IDX[(J, kk)] = bi
            bi += 1

    def vbins(e):
        inst = None
        for J in range(2):
            for kk in range(4 * J, KCNT[J]):
                inst = e.tensor_scalar(
                    at_bins[:, BIN_IDX[(J, kk)] * 512:(BIN_IDX[(J, kk)] + 1) * 512],
                    posq[:, J * 512:(J + 1) * 512],
                    posk[:, kk:kk + 1], None, ALU.is_ge)
        inst.then_inc(SM["v_bin"].h, SM["v_bin"].adv())
    on("vector", vbins)

    slot = 0
    for h in range(H):
        hp = h % 2
        if h == 0:
            wait("sync", SM["g_cckv"], 2)
            wait("sync", SM["v_dqw"], NDQ)
            wait("sync", SM["v_pje"], V_PJE_ALL)
        if h >= 2:
            wait("sync", SM["v_oev"], 2 * (h - 1))
        def kvload(e, h=h, hp=hp):
            for rr in range(2):
                e.dma_start(out=at_kh[hp][:, rr * 1024:(rr + 1) * 1024],
                            in_=kg.ap()[rr * 4096 + h * 128:rr * 4096 + (h + 1) * 128, :]) \
                    .then_inc(SM[f"s_kv{hp}"].h, SM[f"s_kv{hp}"].adv(16))
                e.dma_start(
                    out=at_vh[hp][:, rr * 1024:(rr + 1) * 1024]
                    .rearrange("p (a d) -> p a d", d=128),
                    in_=vg.ap()[rr * 32768 + h * 1024:rr * 32768 + (h + 1) * 1024, :]
                    .rearrange("(a p) d -> p a d", p=P)) \
                    .then_inc(SM[f"s_kv{hp}"].h, SM[f"s_kv{hp}"].adv(16))
        on("sync", kvload)
        kv_sem, kv_cap = SM[f"s_kv{hp}"], SM[f"s_kv{hp}"].v
        for J in range(2):
            hj = 2 * h + J
            last_kk = KCNT[J] - 1
            for kk in range(KCNT[J]):
                sp = slot % 2
                if J == 0 and kk == 0:
                    wait("tensor", kv_sem, kv_cap)
                    wait("tensor", SM["v_pje"],
                         V_PJE_ALL if h == 0 else 128 + 2 * (h + 1))
                if slot >= 2:
                    wait("tensor", SM["s_px"], slot - 1)
                on("tensor", lambda e, h=h, J=J, kk=kk, sp=sp, hp=hp:
                   e.matmul(ps_f[sp][:], at_kh[hp][:, kk * 128:(kk + 1) * 128],
                            qT[:, h * 1024 + J * 512:h * 1024 + (J + 1) * 512],
                            start=True, stop=True)
                   .then_inc(SM["t_s"].h, SM["t_s"].adv()))
                wait("scalar", SM["t_s"], SM["t_s"].v)
                if slot >= 2:
                    wait("scalar", SM["v_pm"], slot - 1)
                    wait("scalar", SM["t_pv"], slot - 1)
                on("scalar", lambda e, sp=sp:
                   e.activation(at_pex[sp][:], ps_f[sp][:], AF.Exp, scale=SM_SCALE)
                   .then_inc(SM["s_px"].h, SM["s_px"].adv()))
                wait("vector", SM["s_px"], SM["s_px"].v)
                if slot == 0:
                    wait("vector", SM["v_bin"], 1)
                if slot >= 2:
                    wait("vector", SM["t_pv"], slot - 1)
                masked = kk >= 4 * J
                zpar = kk % 2
                if kk == 1 and hj >= 1:
                    wait("vector", SM["t_zc"], SM["t_zc"].v - 1 if SM["t_zc"].v >= 1 else 0)
                def vpm(e, J=J, kk=kk, sp=sp, masked=masked, zpar=zpar):
                    src = at_pex[sp]
                    if masked:
                        e.tensor_tensor(at_pm[sp][:], at_pex[sp][:],
                                        at_bins[:, BIN_IDX[(J, kk)] * 512:
                                                (BIN_IDX[(J, kk)] + 1) * 512], ALU.mult)
                        src = at_pm[sp]
                    zdst = zacc[:, (J * 2 + zpar) * 512:(J * 2 + zpar + 1) * 512]
                    if kk == 0:
                        e.tensor_copy(zdst, src[:]).then_inc(SM["v_pm"].h, SM["v_pm"].adv())
                    else:
                        zsrc = zacc[:, (J * 2 + 1 - zpar) * 512:(J * 2 + 2 - zpar) * 512]
                        e.tensor_tensor(zdst, zsrc, src[:], ALU.add) \
                            .then_inc(SM["v_pm"].h, SM["v_pm"].adv())
                on("vector", vpm)
                wait("tensor", SM["v_pm"], SM["v_pm"].v)
                if kk == 0 and hj >= 2:
                    wait("tensor", SM["v_oev"], hj - 1)
                on("tensor", lambda e, J=J, kk=kk, sp=sp, hp=hp, masked=masked,
                   last=(kk == last_kk):
                   e.matmul(ps_f[2 + J][:], at_vh[hp][:, kk * 128:(kk + 1) * 128],
                            (at_pm[sp] if masked else at_pex[sp])[:],
                            start=(kk == 0), stop=last)
                   .then_inc(SM["t_pv"].h, SM["t_pv"].adv()))
                slot += 1
            zlast = (KCNT[J] - 1) % 2
            if hj >= 1:
                wait("tensor", SM["v_rz"], hj)
            on("tensor", lambda e, J=J, zlast=zlast:
               e.matmul(ps_z[:], ones_col[:],
                        zacc[:, (J * 2 + zlast) * 512:(J * 2 + zlast + 1) * 512],
                        start=True, stop=True).then_inc(SM["t_zc"].h, SM["t_zc"].adv()))
            wait("vector", SM["t_zc"], SM["t_zc"].v)
            if hj >= 1:
                wait("vector", SM["t_rzb"], SM["t_rzb"].v)
            on("vector", lambda e: e.reciprocal(rz, ps_z[:])
               .then_inc(SM["v_rz"].h, SM["v_rz"].adv()))
            wait("tensor", SM["v_rz"], SM["v_rz"].v)
            if hj >= 1:
                wait("tensor", SM["v_oev"], hj)
            on("tensor", lambda e: e.matmul(ps_f[4][:], ones_row[:], rz[0:1, :],
                                            start=True, stop=True)
               .then_inc(SM["t_rzb"].h, SM["t_rzb"].adv()))
            wait("vector", SM["t_rzb"], SM["t_rzb"].v)
            if hj == 0:
                wait("vector", SM["s_xnf"], 16)
            def voev(e, h=h, J=J):
                e.tensor_copy(rzrep, ps_f[4][:])
                e.tensor_tensor(outT[:, h * 1024 + J * 512:h * 1024 + (J + 1) * 512],
                                ps_f[2 + J][:], rzrep, ALU.mult) \
                    .then_inc(SM["v_oev"].h, SM["v_oev"].adv())
            on("vector", voev)
    T_S_ALL = SM["t_s"].v
    V_OEV_ALL = SM["v_oev"].v
    T_PV_ALL = SM["t_pv"].v

    set_phase(50)
    op_store_caps = []
    for ds in range(8):
        par = ds % 2
        if ds == 0:
            wait("gpsimd", SM["v_oev"], V_OEV_ALL)
        if ds >= 2:
            wait("gpsimd", SM["t_op"], (ds - 1) * 8)
        def gwo(e, ds=ds, par=par):
            e.dma_start(out=wo_buf[par].rearrange("p (a w) -> p a w", w=512),
                        in_=wT["o"][:, ds * 512:(ds + 1) * 512]
                        .rearrange("(a p) w -> p a w", p=P)) \
                .then_inc(SM[f"g_wo{par}"].h, SM[f"g_wo{par}"].adv(16))
        on("gpsimd", gwo)
        wo_sem, wo_cap = SM[f"g_wo{par}"], SM[f"g_wo{par}"].v
        for tc in range(8):
            g = ds * 8 + tc
            bank = g % 4
            wait("tensor", wo_sem, wo_cap)
            if g == 0:
                wait("tensor", SM["v_oev"], V_OEV_ALL)
            if g >= 4:
                wait("tensor", SM["v_ope"], g - 3)
            def top(e, tc=tc, par=par, bank=bank):
                inst = None
                for cc in range(NCC):
                    inst = e.matmul(ps_f[bank][:],
                                    outT[:, cc * 1024 + tc * 128:cc * 1024 + (tc + 1) * 128],
                                    wo_buf[par][:, cc * 512:(cc + 1) * 512],
                                    start=(cc == 0), stop=(cc == NCC - 1))
                inst.then_inc(SM["t_op"].h, SM["t_op"].adv())
            on("tensor", top)
            wait("vector", SM["t_op"], SM["t_op"].v)
            spar = g % 2
            if g >= 2:
                wait("vector", *op_store_caps[g - 2])
            on("vector", lambda e, spar=spar, bank=bank:
               e.tensor_copy(op_stage[spar], ps_f[bank][:])
               .then_inc(SM["v_ope"].h, SM["v_ope"].adv()))
            wait("scalar", SM["v_ope"], SM["v_ope"].v)
            on("scalar", lambda e, ds=ds, tc=tc, spar=spar:
               e.dma_start(out=attn_dram.ap()[tc * 128:(tc + 1) * 128,
                                              ds * 512:(ds + 1) * 512],
                           in_=op_stage[spar])
               .then_inc(SM[f"s_opst{spar}"].h, SM[f"s_opst{spar}"].adv(16)))
            op_store_caps.append((SM[f"s_opst{spar}"], SM[f"s_opst{spar}"].v))
    T_OP_ALL = SM["t_op"].v
    OP_ST_ALL = [(SM["s_opst0"], SM["s_opst0"].v), (SM["s_opst1"], SM["s_opst1"].v)]

    set_phase(60)
    wait("sync", SM["t_s"], T_S_ALL)
    on("sync", lambda e: e.dma_start(out=xn_f2[:].rearrange("p (a t) -> p a t", t=TL),
                                     in_=xnf_dram.ap().rearrange("(a p) t -> p a t", p=P))
       .then_inc(SM["s_xnfl"].h, SM["s_xnfl"].adv(16)))

    gg_v = gg.ap().rearrange("(r a b p) f -> r a b p f", r=8, a=NCC, b=11, p=P)
    ug_v = ug.ap().rearrange("(r a b p) f -> r a b p f", r=8, a=NCC, b=11, p=P)
    x3_store_caps = []
    for fc in range(NFC):
        r, fb = fc // 11, fc % 11
        par = fc % 2
        if fc == 0:
            wait("sync", SM["g_ccw"], 3)
            wait("sync", SM["t_op"], T_OP_ALL)
        if fc >= 2:
            wait("sync", SM["t_y"], (fc - 1) * 4)
        def fgl(e, r=r, fb=fb, par=par):
            e.dma_start(out=fg_w[par].rearrange("p (a f) -> p a f", f=128),
                        in_=gg_v[r, :, fb].rearrange("a p f -> p a f")) \
                .then_inc(SM[f"s_fg{par}"].h, SM[f"s_fg{par}"].adv(16))
            e.dma_start(out=fg_w[2 + par].rearrange("p (a f) -> p a f", f=128),
                        in_=ug_v[r, :, fb].rearrange("a p f -> p a f")) \
                .then_inc(SM[f"s_fg{par}"].h, SM[f"s_fg{par}"].adv(16))
        on("sync", fgl)
        wait("tensor", SM[f"s_fg{par}"], SM[f"s_fg{par}"].v)
        if fc == 0:
            wait("tensor", SM["s_xnfl"], 16)
        if fc >= 1:
            wait("tensor", SM["v_x3"], (fc - 1) * 2 + 2)
            wait("tensor", SM["s_y2c"], (fc - 1) * 2 + 2)
        def tgu(e, par=par):
            inst = None
            for tt in range(2):
                for cc in range(NCC):
                    e.matmul(ps_f[0 + tt][:], fg_w[par][:, cc * 128:(cc + 1) * 128],
                             xn_f2[:, cc * 1024 + tt * 512:cc * 1024 + (tt + 1) * 512],
                             start=(cc == 0), stop=(cc == NCC - 1))
                for cc in range(NCC):
                    inst = e.matmul(ps_f[2 + tt][:], fg_w[2 + par][:, cc * 128:(cc + 1) * 128],
                                    xn_f2[:, cc * 1024 + tt * 512:cc * 1024 + (tt + 1) * 512],
                                    start=(cc == 0), stop=(cc == NCC - 1))
            inst.then_inc(SM["t_y"].h, SM["t_y"].adv(4))
        on("tensor", tgu)
        for tt in range(2):
            wait("scalar", SM["t_y"], SM["t_y"].v)
            if fc >= 1:
                wait("scalar", SM["v_x3"], SM["v_x3"].v - 1)
            on("scalar", lambda e, tt=tt: e.activation(fy2[tt][:], ps_f[2 + tt][:], AF.Copy)
               .then_inc(SM["s_y2c"].h, SM["s_y2c"].adv()))
            wait("vector", SM["s_y2c"], SM["s_y2c"].v)
            if len(x3_store_caps) >= 2:
                wait("vector", *x3_store_caps[-2])
            on("vector", lambda e, tt=tt:
               e.scalar_tensor_tensor(fx3[tt][:], ps_f[0 + tt][:], 0.0, fy2[tt][:],
                                      ALU.max, ALU.mult)
               .then_inc(SM["v_x3"].h, SM["v_x3"].adv()))
            wait("scalar", SM["v_x3"], SM["v_x3"].v)
            on("scalar", lambda e, fc=fc, tt=tt:
               e.dma_start(out=x3_dram.ap()[fc * P:(fc + 1) * P, tt * 512:(tt + 1) * 512],
                           in_=fx3[tt][:]).then_inc(SM[f"s_x3st{tt}"].h, SM[f"s_x3st{tt}"].adv(16)))
            x3_store_caps.append((SM[f"s_x3st{tt}"], SM[f"s_x3st{tt}"].v))
    X3_ST_ALL = [(SM["s_x3st0"], SM["s_x3st0"].v), (SM["s_x3st1"], SM["s_x3st1"].v)]

    set_phase(70)
    dg_v = dg.ap()
    nload = 0
    out_store_caps = []
    fd_consume = []
    for o in range(8):
        for tch in range(2):
            oi = o * 2 + tch
            gpar = 0
            for fc in range(NFC):
                if fc % 8 == 0:
                    gpar = nload % 2
                    if oi == 0 and fc == 0:
                        for s, v in X3_ST_ALL:
                            wait("sync", s, v)
                    if nload >= 2:
                        wait("sync", SM["t_d"], fd_consume[nload - 2])
                    n = min(8, NFC - fc)
                    def fdl(e, o=o, fc=fc, gpar=gpar, n=n):
                        e.dma_start(
                            out=fd_w[gpar][:, 0:n * 512].rearrange("p (a w) -> p a w", w=512),
                            in_=dg_v[o * DFFP + fc * P:o * DFFP + (fc + n) * P, :]
                            .rearrange("(a p) w -> p a w", p=P)) \
                            .then_inc(SM[f"s_fd{gpar}"].h, SM[f"s_fd{gpar}"].adv(16))
                    on("sync", fdl)
                    fd_sem, fd_cap = SM[f"s_fd{gpar}"], SM[f"s_fd{gpar}"].v
                    fd_consume.append(oi * NFC + min(fc + 8, NFC))
                    nload += 1
                xpar = fc % 2
                if SM["t_d"].v >= 1 and oi * NFC + fc >= 2:
                    wait("sync", SM["t_d"], SM["t_d"].v - 1)
                def fxl(e, fc=fc, tch=tch, xpar=xpar):
                    e.dma_start(out=fd_x3[xpar][:],
                                in_=x3_dram.ap()[fc * P:(fc + 1) * P,
                                                 tch * 512:(tch + 1) * 512]) \
                        .then_inc(SM[f"s_xr{xpar}"].h, SM[f"s_xr{xpar}"].adv(16))
                on("sync", fxl)
                wait("tensor", SM[f"s_xr{xpar}"], SM[f"s_xr{xpar}"].v)
                if fc % 8 == 0:
                    wait("tensor", fd_sem, fd_cap)
                if fc == 0 and oi >= 1:
                    wait("tensor", SM["v_de"], (oi - 1) * 4 + 4)
                def tdm(e, fc=fc, gpar=gpar, xpar=xpar):
                    woff = (fc % 8) * 512
                    inst = None
                    for tq in range(4):
                        inst = e.matmul(ps_f[tq][:],
                                        fd_x3[xpar][:, tq * 128:(tq + 1) * 128],
                                        fd_w[gpar][:, woff:woff + 512],
                                        start=(fc == 0), stop=(fc == NFC - 1))
                    inst.then_inc(SM["t_d"].h, SM["t_d"].adv())
                on("tensor", tdm)
            T_D_CUR = SM["t_d"].v
            for tq in range(4):
                tglob = tch * 4 + tq
                epar = tq % 2
                ei = oi * 4 + tq
                if oi == 0 and tq == 0:
                    for s, v in OP_ST_ALL:
                        wait("sync", s, v)
                if ei >= 2:
                    wait("sync", SM["v_de"], ei - 1)
                def aol(e, o=o, tglob=tglob, epar=epar):
                    e.dma_start(out=fo_at[epar][:],
                                in_=attn_dram.ap()[tglob * 128:(tglob + 1) * 128,
                                                   o * 512:(o + 1) * 512]) \
                        .then_inc(SM[f"s_ao{epar}"].h, SM[f"s_ao{epar}"].adv(16))
                    e.dma_start(out=fo_x[epar][:],
                                in_=x_in[tglob * 128:(tglob + 1) * 128,
                                         o * 512:(o + 1) * 512]) \
                        .then_inc(SM[f"s_ao{epar}"].h, SM[f"s_ao{epar}"].adv(16))
                on("sync", aol)
                ao_sem, ao_cap = SM[f"s_ao{epar}"], SM[f"s_ao{epar}"].v
                wait("vector", SM["t_d"], T_D_CUR)
                wait("vector", ao_sem, ao_cap)
                if ei >= 2:
                    wait("vector", *out_store_caps[ei - 2])
                def vde(e, tq=tq, epar=epar):
                    e.tensor_tensor(fo_a[epar][:], ps_f[tq][:], fo_at[epar][:], ALU.add)
                    e.tensor_tensor(fo_b[epar][:], fo_a[epar][:], fo_x[epar][:], ALU.add) \
                        .then_inc(SM["v_de"].h, SM["v_de"].adv())
                on("vector", vde)
                wait("scalar", SM["v_de"], SM["v_de"].v)
                on("scalar", lambda e, o=o, tglob=tglob, epar=epar:
                   e.dma_start(out=out_ext[tglob * 128:(tglob + 1) * 128,
                                           o * 512:(o + 1) * 512],
                               in_=fo_b[epar][:])
                   .then_inc(SM[f"s_outst{epar}"].h, SM[f"s_outst{epar}"].adv(16)))
                out_store_caps.append((SM[f"s_outst{epar}"], SM[f"s_outst{epar}"].v))
    wait("scalar", SM["s_outst0"], SM["s_outst0"].v)
    wait("scalar", SM["s_outst1"], SM["s_outst1"].v)

    Ctr.TRACE = None
    with stk:
        with nc.Block() as block:
            class _InstW:
                __slots__ = ("inst", "owner")

                def __init__(self, inst, owner):
                    self.inst = inst
                    self.owner = owner

                def then_inc(self, h, n):
                    self.owner._recorded(h, n)
                    self.inst = self.inst.then_inc(h, n)
                    return self

                def __getattr__(self, k):
                    return getattr(self.inst, k)

            class _Fence:

                def __init__(self, eng, sem, semvals):
                    self._eng = eng
                    self._sem = sem
                    self._n = 0
                    self._semvals = semvals
                    self._pend = None

                def _recorded(self, h, n):
                    self._semvals[id(h)] = self._semvals.get(id(h), 0) + n
                    if self._pend is not None:
                        self._pend = (self._pend[0], (h, self._semvals[id(h)]))

                def _fence(self):
                    p = self._pend
                    if p is None:
                        return
                    self._pend = None
                    iw, rec = p
                    if rec is None:
                        self._n += 1
                        iw.inst.then_inc(self._sem, 1)
                        self._eng.wait_ge(self._sem, self._n)
                    else:
                        self._eng.wait_ge(rec[0], rec[1])

                def wait_ge(self, h, v):
                    self._fence()
                    return self._eng.wait_ge(h, v)

                def dma_start(self, *a, **k):
                    self._fence()
                    return _InstW(self._eng.dma_start(*a, **k), self)

                def __getattr__(self, name):
                    f = getattr(self._eng, name)
                    if not callable(f):
                        return f

                    def g(*a, **k):
                        self._fence()
                        inst = f(*a, **k)
                        iw = _InstW(inst, self)
                        self._pend = (iw, None)
                        return iw
                    return g

            semvals_shared = {}

            def replay(eng, name):
                if name in ("vector", "scalar"):
                    eng = _Fence(eng, SM["v_ssync" if name == "vector" else "s_ssync"].h,
                                 semvals_shared)
                for e, fn, ph in prog:
                    if e == name and ph <= phase_limit:
                        fn(eng)
                if hasattr(eng, "_fence"):
                    eng._fence()

            @block.sync
            def _(eng):
                replay(eng, "sync")

            @block.gpsimd
            def _(eng):
                replay(eng, "gpsimd")

            @block.vector
            def _(eng):
                replay(eng, "vector")

            @block.scalar
            def _(eng):
                replay(eng, "scalar")

            @block.tensor
            def _(eng):
                replay(eng, "tensor")

    return nc


_NC_CACHE = None


def _get_nc():
    global _NC_CACHE
    if _NC_CACHE is None:
        _NC_CACHE = build_nc()
    return _NC_CACHE


def kernel(**inputs):
    x = np.ascontiguousarray(np.asarray(inputs["L0_attn_x"], dtype=np.float32))
    B, T, _ = x.shape
    wqT = np.ascontiguousarray(np.asarray(inputs["L0_attn_wq"], np.float32).T)
    wkT = np.ascontiguousarray(np.asarray(inputs["L0_attn_wk"], np.float32).T)
    wvT = np.ascontiguousarray(np.asarray(inputs["L0_attn_wv"], np.float32).T)
    woT = np.ascontiguousarray(np.asarray(inputs["L0_attn_wo"], np.float32).T)

    g_codes = np.asarray(inputs["L0_ffn_g_codes"], np.int32)
    u_codes = np.asarray(inputs["L0_ffn_u_codes"], np.int32)
    d_codes = np.asarray(inputs["L0_ffn_d_codes"], np.int32)
    g_amax = np.asarray(inputs["L0_ffn_g_absmax"], np.float32)
    u_amax = np.asarray(inputs["L0_ffn_u_absmax"], np.float32)
    d_amax = np.asarray(inputs["L0_ffn_d_absmax"], np.float32)
    gaT = np.ascontiguousarray(np.asarray(inputs["L0_ffn_g_lora_a"], np.float32).T)
    uaT = np.ascontiguousarray(np.asarray(inputs["L0_ffn_u_lora_a"], np.float32).T)
    daT = np.zeros((16, DFFP), np.float32)
    daT[:, :DFF] = np.asarray(inputs["L0_ffn_d_lora_a"], np.float32).T
    gb = np.asarray(inputs["L0_ffn_g_lora_b"], np.float32)
    ub = np.asarray(inputs["L0_ffn_u_lora_b"], np.float32)
    db = np.asarray(inputs["L0_ffn_d_lora_b"], np.float32)

    in_maps = []
    for r in range(8):
        b, half = r // 2, r % 2
        xs = np.ascontiguousarray(x[b, half * TL:(half + 1) * TL, :])
        pos = np.ascontiguousarray(
            np.arange(half * TL, (half + 1) * TL, dtype=np.float32)[None, :])
        f0 = r * FSP
        f1 = min(f0 + FSP, DFF)
        nsl = max(0, f1 - f0)
        gsl = np.zeros((C, FSP), np.int32)
        usl = np.zeros((C, FSP), np.int32)
        gam = np.zeros((C, FSP), np.float32)
        uam = np.zeros((C, FSP), np.float32)
        gbs = np.zeros((16, FSP), np.float32)
        ubs = np.zeros((16, FSP), np.float32)
        if nsl > 0:
            gsl[:, :nsl] = g_codes[f0:f1, :].T
            usl[:, :nsl] = u_codes[f0:f1, :].T
            gam[:, :nsl] = np.repeat(g_amax[f0:f1, :].T, 64, axis=0)
            uam[:, :nsl] = np.repeat(u_amax[f0:f1, :].T, 64, axis=0)
            gbs[:, :nsl] = gb[:, f0:f1]
            ubs[:, :nsl] = ub[:, f0:f1]
        o0 = r * OSL
        dsl = np.zeros((DFFP, OSL), np.int32)
        dsl[:DFF, :] = d_codes[o0:o0 + OSL, :].T
        dam = np.zeros((DFFP, OSL), np.float32)
        dam[:DFF, :] = np.repeat(d_amax[o0:o0 + OSL, :].T, 64, axis=0)
        dbs = np.ascontiguousarray(db[:, o0:o0 + OSL])
        in_maps.append(dict(
            x_in=xs, pos_in=pos, wqT=wqT, wkT=wkT, wvT=wvT, woT=woT,
            gcodesT=np.ascontiguousarray(gsl), ucodesT=np.ascontiguousarray(usl),
            dcodesT=dsl, gamaxR=gam, uamaxR=uam, damaxR=dam,
            gaT_in=gaT, uaT_in=uaT, daT_in=np.ascontiguousarray(daT),
            gb_in=gbs, ub_in=ubs, db_in=dbs))

    nc = _get_nc()
    res = run_bass_kernel_spmd(nc, in_maps, core_ids=list(range(8)))
    out = np.empty((B, T, C), np.float32)
    for r in range(8):
        b, half = r // 2, r % 2
        out[b, half * TL:(half + 1) * TL, :] = res.results[r]["out"]
    return out


# version 13
# speedup vs baseline: 1.1685x; 1.1685x over previous
import math
from contextlib import ExitStack

import numpy as np
import concourse.bass as bass
import concourse.mybir as mybir
from concourse.bass_utils import run_bass_kernel_spmd

F32 = mybir.dt.float32
BF16 = mybir.dt.bfloat16
I32 = mybir.dt.int32
U16 = mybir.dt.uint16
AF = mybir.ActivationFunctionType
ALU = mybir.AluOpType

P = 128
TL = 1024
C = 4096
NCC = 32
H = 32
HD = 128
DFF = 11008
DFFP = 11264
NFC = 88
FSP = 1408
OSL = 512
KCNT = [12, 16]
NF4 = [-1.0, -0.6961928009986877, -0.5250730514526367, -0.39491748809814453,
       -0.28444138169288635, -0.18477343022823334, -0.09105003625154495, 0.0,
       0.07958029955625534, 0.16093020141124725, 0.2461123019456863,
       0.33791524171829224, 0.44070982933044434, 0.5626170039176941,
       0.7229568362236023, 1.0]
TWO_PI = 2.0 * math.pi
SM_SCALE = 1.0 / math.sqrt(HD)


class Ctr:
    TRACE = None
    CUR_ENG = [None]

    def __init__(self, h, name=""):
        self.h = h
        self.v = 0
        self.name = name

    def inc(self, n=1):
        self.v += n
        return self.v

    def adv(self, n=1):
        self.v += n
        if Ctr.TRACE is not None and Ctr.CUR_ENG[0] is not None:
            Ctr.TRACE.append((Ctr.CUR_ENG[0], "i", self.name, n))
        return n


def build_nc(phase_limit=99, audit=None):
    nc = bass.Bass("TRN2", target_bir_lowering=False, debug=False)

    def din(name, shape, dt):
        return nc.dram_tensor(name, shape, dt, kind="ExternalInput").ap()

    x_in = din("x_in", [TL, C], F32)
    pos_in = din("pos_in", [1, TL], F32)
    wT = {"k": din("wkT", [C, C], F32), "v": din("wvT", [C, C], F32),
          "q": din("wqT", [C, C], F32), "o": din("woT", [C, C], F32)}
    gcodesT = din("gcodesT", [C, FSP], I32)
    ucodesT = din("ucodesT", [C, FSP], I32)
    dcodesT = din("dcodesT", [DFFP, OSL], I32)
    gamaxR = din("gamaxR", [C, FSP], F32)
    uamaxR = din("uamaxR", [C, FSP], F32)
    damaxR = din("damaxR", [DFFP, OSL], F32)
    gaT_in = din("gaT_in", [16, C], F32)
    uaT_in = din("uaT_in", [16, C], F32)
    daT_in = din("daT_in", [16, DFFP], F32)
    gb_in = din("gb_in", [16, FSP], F32)
    ub_in = din("ub_in", [16, FSP], F32)
    db_in = din("db_in", [16, OSL], F32)
    out_ext = nc.dram_tensor("out", [TL, C], F32, kind="ExternalOutput").ap()

    kcon = nc.dram_tensor("kcon", [H * HD, TL], BF16)
    vcon = nc.dram_tensor("vcon", [H * TL, HD], BF16)
    kg = nc.dram_tensor("kg", [2 * H * HD, TL], BF16)
    vg = nc.dram_tensor("vg", [2 * H * TL, HD], BF16)
    gcon = nc.dram_tensor("gcon", [NCC * 11 * P, P], BF16)
    ucon = nc.dram_tensor("ucon", [NCC * 11 * P, P], BF16)
    dcon = nc.dram_tensor("dcon", [DFFP, OSL], BF16)
    gg = nc.dram_tensor("gg", [8 * NCC * 11 * P, P], BF16)
    ug = nc.dram_tensor("ug", [8 * NCC * 11 * P, P], BF16)
    dg = nc.dram_tensor("dg", [8 * DFFP, OSL], BF16)
    xnf_dram = nc.dram_tensor("xnf_dram", [C, TL], BF16)
    attn_dram = nc.dram_tensor("attn_dram", [TL, C], F32)
    x3_dram = nc.dram_tensor("x3_dram", [DFFP, TL], BF16)
    lorab_dram = nc.dram_tensor("lorab_dram", [16, 2 * C + DFFP], BF16)

    stk = ExitStack()
    sb = lambda n, s, d: stk.enter_context(nc.sbuf_tensor(n, s, d))
    pst = lambda n, s, d: stk.enter_context(nc.psum_tensor(n, s, d))

    BIG1 = sb("BIG1", [P, 32768], BF16)
    BIG2 = sb("BIG2", [P, 32768], BF16)
    U = sb("U", [P, 32768], BF16)
    posq = sb("posq", [P, TL], F32)
    sincos = sb("sincos", [P, 2 * TL], BF16)
    ident = sb("ident", [P, P], BF16)
    posk = sb("posk", [P, 16], F32)
    theta = sb("theta", [P, 1], F32)
    ones_col = sb("ones_col", [P, 1], F32)
    ones_row = sb("ones_row", [1, P], F32)
    consts = sb("consts", [P, 4], F32)
    stats = sb("stats", [P, 8], F32)
    gb_sl = sb("gb_sl", [16, FSP], BF16)
    ub_sl = sb("ub_sl", [16, FSP], BF16)
    db_sl = sb("db_sl", [16, OSL], BF16)

    def uf32(o, n):
        return U[:, o:o + n].bitcast(F32)

    ir = U[:, 14336:14592].bitcast(I32)
    ic = U[:, 14592:14848].bitcast(I32)
    possb = U[0:1, 14848:16896].bitcast(F32)
    zacc = uf32(22528, 4096)
    rz = U[0:1, 26624:27648].bitcast(F32)
    rzrep = uf32(27648, 1024)
    op_stage = [sincos[:, i * 1024:(i + 1) * 1024].bitcast(F32) for i in range(2)]
    xtok = uf32(0, 8192)
    xn_tok = [U[:, 8192:12288], U[:, 12288:16384]]
    lpre = uf32(8192, 4096)
    lpre_b = U[:, 12288:14336]
    dq_codes = [U[:, 16384 + i * 1024:16384 + (i + 1) * 1024].bitcast(I32) for i in range(2)]
    dq_cu16 = U[:, 18432:18944].bitcast(U16)
    dq_m0 = U[:, 18944:19456].bitcast(U16)
    dq_m0b = U[:, 19456:19968]
    dq_m1 = U[:, 19968:20480].bitcast(U16)
    dq_m2 = U[:, 20480:20992].bitcast(U16)
    dq_m3 = U[:, 20992:21504].bitcast(U16)
    dq_b8 = U[:, 21504:25600]
    dq_t4 = U[:, 25600:27648]
    dq_u2 = U[:, 27648:28672]
    dq_v1 = U[:, 28672:29184]
    dq_arep = [uf32(29184 + i * 1024, 1024) for i in range(2)]
    dq_w = [U[:, 31232 + i * 512:31232 + (i + 1) * 512] for i in range(2)]
    dq_lt = [U[:, 32256 + i * 128:32256 + (i + 1) * 128] for i in range(2)]
    kev = [U[:, 8192 + i * 512:8192 + (i + 1) * 512] for i in range(2)]
    vev = [U[:, 9216 + i * 512:9216 + (i + 1) * 512] for i in range(2)]
    vtr = [U[:, 10240 + i * 512:10240 + (i + 1) * 512] for i in range(2)]
    rt1 = uf32(11264, 1024)
    rt2 = uf32(13312, 1024)
    strip_qo = [U[:, i * 4096:(i + 1) * 4096] for i in range(2)]
    at_bins = U[:, 0:12288]
    at_kh = [U[:, 12288 + i * 2048:12288 + (i + 1) * 2048] for i in range(2)]
    at_vh = [U[:, 16384 + i * 2048:16384 + (i + 1) * 2048] for i in range(2)]
    at_pex = [U[:, 20480 + i * 512:20480 + (i + 1) * 512] for i in range(2)]
    at_pm = [U[:, 21504 + i * 512:21504 + (i + 1) * 512] for i in range(2)]
    wo_buf = [U[:, i * 16384:(i + 1) * 16384] for i in range(2)]
    fg_w = [U[:, i * 4096:(i + 1) * 4096] for i in range(4)]
    fy2 = [U[:, 16384 + i * 512:16384 + (i + 1) * 512] for i in range(2)]
    fx3 = [U[:, 17408 + i * 512:17408 + (i + 1) * 512] for i in range(2)]
    fd_w = [U[:, i * 4096:(i + 1) * 4096] for i in range(2)]
    fd_x3 = [U[:, 8192 + i * 512:8192 + (i + 1) * 512] for i in range(2)]
    fo_at = [uf32(9216 + i * 1024, 1024) for i in range(2)]
    fo_x = [uf32(11264 + i * 1024, 1024) for i in range(2)]
    fo_a = [uf32(13312 + i * 1024, 1024) for i in range(2)]
    fo_b = [uf32(18432 + i * 1024, 1024) for i in range(2)]

    qT = BIG2
    xn_f = BIG1
    outT = BIG1
    xn_f2 = BIG2
    strip_kv = [BIG2[:, i * 4096:(i + 1) * 4096] for i in range(2)]
    rope_pt = BIG2[:, 24576:26624].bitcast(F32)
    rope_kf = BIG2[:, 26624:28672].bitcast(F32)
    rope_ki = BIG2[:, 28672:30720].bitcast(I32)
    rope_rr = BIG2[:, 30720:32768].bitcast(F32)

    ps_f = [pst(f"ps{i}", [P, 512], F32) for i in range(5)]
    ps_z = pst("ps_z", [1, 512], F32)
    ps_bb = [pst("ps_bA", [P, 1024], BF16), pst("ps_bB", [P, 1024], BF16)]

    names = ["s_pos", "s_x", "s_lorain", "s_lorabf",
             "s_cod0", "s_cod1", "s_arep0", "s_arep1", "s_lt0", "s_lt1",
             "s_kv0", "s_kv1", "s_fg0", "s_fg1", "s_fd0", "s_fd1",
             "s_xr0", "s_xr1", "s_ao0", "s_ao1", "s_xnf", "s_xnfl",
             "g_init", "g_st0", "g_st1", "g_wo0", "g_wo1", "g_cckv", "g_ccw",
             "s_dst0", "s_dst1", "s_kst0", "s_kst1", "s_vst0", "s_vst1",
             "s_opst0", "s_opst1", "s_x3st0", "s_x3st1", "s_outst0", "s_outst1",
             "q_st", "q_out", "v_self", "s_self",
             "v_ln", "s_lns", "s_sqrt", "t_tr", "v_tre",
             "v_dqm", "s_dqb", "t_dql", "v_dqw",
             "t_posb", "v_posb", "v_rtab", "s_sin", "v_pre",
             "t_pj", "v_pje", "t_vtr", "v_vte",
             "t_s", "s_px", "v_pm", "t_pv", "t_zc", "v_rz", "t_rzb", "v_oev",
             "v_bin", "t_op", "v_ope",
             "t_y", "s_y2c", "v_x3", "t_d", "v_de", "t_kvf", "t_strf"]
    SM = {n: Ctr(stk.enter_context(nc.semaphore(n)), n) for n in names}

    prog = []
    cur_phase = [0]
    Ctr.TRACE = audit

    class _MockInst:
        def then_inc(self, h, n):
            return self

    class _MockEng:
        def __getattr__(self, name):
            def f(*a, **k):
                return _MockInst()
            return f

    _MOCK = _MockEng()

    def set_phase(n):
        cur_phase[0] = n

    def on(eng, fn):
        Ctr.CUR_ENG[0] = eng
        fn(_MOCK)
        Ctr.CUR_ENG[0] = None
        prog.append((eng, fn, cur_phase[0]))

    def wait(eng, ctr, val):
        if val > 0:
            if Ctr.TRACE is not None:
                Ctr.TRACE.append((eng, "w", ctr.name, val))
            prog.append((eng, lambda e, h=ctr.h, v=val: e.wait_ge(h, v), cur_phase[0]))

    def g0(e):
        e.iota(ir, [[1, P]], base=0, channel_multiplier=0)
        e.iota(ic, [[0, P]], base=0, channel_multiplier=1)
        e.iota(posk[:], [[P, 16]], base=0, channel_multiplier=1,
               allow_small_or_imprecise_dtypes=True)
        e.iota(theta[0:64, :], [[0, 1]], base=0, channel_multiplier=1,
               allow_small_or_imprecise_dtypes=True)
        e.iota(theta[64:128, :], [[0, 1]], base=0, channel_multiplier=1,
               allow_small_or_imprecise_dtypes=True)
        e.memset(ones_col[:], 1.0)
        e.memset(ones_row[:], 1.0)
        e.memset(consts[:, 0:1], 1e-5)
        e.memset(consts[:, 1:2], -math.pi).then_inc(SM["g_init"].h, SM["g_init"].adv())
    on("gpsimd", g0)
    wait("vector", SM["g_init"], 1)
    on("vector", lambda e: e.tensor_tensor(ident[:], ir, ic, ALU.is_equal))
    wait("scalar", SM["g_init"], 1)
    on("scalar", lambda e: e.activation(theta[:], theta[:], AF.Exp,
                                        scale=-math.log(10000.0) / 64.0)
       .then_inc(SM["s_self"].h, SM["s_self"].adv()))
    on("sync", lambda e: e.dma_start(out=possb, in_=pos_in)
       .then_inc(SM["s_pos"].h, SM["s_pos"].adv(16)))
    wait("tensor", SM["s_pos"], 16)
    wait("tensor", SM["g_init"], 1)
    on("tensor", lambda e: e.matmul(ps_f[4][:], ones_row[:], possb[0:1, 0:512],
                                    start=True, stop=True)
       .then_inc(SM["t_posb"].h, SM["t_posb"].adv()))
    wait("vector", SM["t_posb"], 1)
    on("vector", lambda e: e.tensor_copy(posq[:, 0:512], ps_f[4][:])
       .then_inc(SM["v_posb"].h, SM["v_posb"].adv()))
    wait("tensor", SM["v_posb"], 1)
    on("tensor", lambda e: e.matmul(ps_f[4][:], ones_row[:], possb[0:1, 512:1024],
                                    start=True, stop=True)
       .then_inc(SM["t_posb"].h, SM["t_posb"].adv()))
    wait("vector", SM["t_posb"], 2)
    on("vector", lambda e: e.tensor_copy(posq[:, 512:1024], ps_f[4][:])
       .then_inc(SM["v_posb"].h, SM["v_posb"].adv()))
    wait("vector", SM["s_self"], 1)

    def vsin(e):
        e.tensor_scalar(rope_pt[:], posq[:], theta[:, 0:1], None, ALU.mult)
        e.tensor_scalar(rope_kf[:], rope_pt[:], 1.0 / TWO_PI, None, ALU.mult)
        e.tensor_copy(rope_ki[:], rope_kf[:])
        e.tensor_copy(rope_kf[:], rope_ki[:])
        e.scalar_tensor_tensor(rope_rr[:], rope_kf[:], -TWO_PI, rope_pt[:],
                               ALU.mult, ALU.add).then_inc(SM["v_rtab"].h, SM["v_rtab"].adv())
    on("vector", vsin)
    wait("scalar", SM["v_rtab"], 1)
    on("scalar", lambda e: e.activation(sincos[:, 0:TL], rope_rr[:], AF.Sin)
       .then_inc(SM["s_sin"].h, SM["s_sin"].adv()))
    wait("vector", SM["s_sin"], 1)

    def vcos(e):
        e.tensor_scalar(rope_kf[:], rope_pt[:], 1.0, math.pi / 2.0, ALU.mult, ALU.add)
        e.tensor_scalar(rope_rr[:], rope_kf[:], 1.0 / TWO_PI, None, ALU.mult)
        e.tensor_copy(rope_ki[:], rope_rr[:])
        e.tensor_copy(rope_pt[:], rope_ki[:])
        e.scalar_tensor_tensor(rope_rr[:], rope_pt[:], -TWO_PI, rope_kf[:],
                               ALU.mult, ALU.add).then_inc(SM["v_rtab"].h, SM["v_rtab"].adv())
    on("vector", vcos)
    wait("scalar", SM["v_rtab"], 2)
    on("scalar", lambda e: e.activation(sincos[:, TL:2 * TL], rope_rr[:], AF.Sin)
       .then_inc(SM["s_sin"].h, SM["s_sin"].adv()))

    for (src, dst_off, width) in [(gaT_in, 0, C), (uaT_in, C, C), (daT_in, 2 * C, DFFP)]:
        for c0 in range(0, width, 2048):
            w = min(2048, width - c0)
            on("sync", lambda e, src=src, c0=c0, w=w:
               e.dma_start(out=lpre[0:16, 0:w], in_=src[:, c0:c0 + w])
               .then_inc(SM["s_lorain"].h, SM["s_lorain"].adv(16)))
            wait("vector", SM["s_lorain"], SM["s_lorain"].v)
            on("vector", lambda e, w=w: e.tensor_copy(lpre_b[0:16, 0:w], lpre[0:16, 0:w])
               .then_inc(SM["v_pre"].h, SM["v_pre"].adv()))
            wait("scalar", SM["v_pre"], SM["v_pre"].v)
            on("scalar", lambda e, dst_off=dst_off, c0=c0, w=w:
               e.dma_start(out=lorab_dram.ap()[:, dst_off + c0:dst_off + c0 + w],
                           in_=lpre_b[0:16, 0:w])
               .then_inc(SM["s_lorabf"].h, SM["s_lorabf"].adv(16)))
            wait("sync", SM["s_lorabf"], SM["s_lorabf"].v)
    for (src, dstt, width) in [(gb_in, gb_sl, FSP), (ub_in, ub_sl, FSP), (db_in, db_sl, OSL)]:
        on("sync", lambda e, src=src, width=width:
           e.dma_start(out=lpre[0:16, 0:width], in_=src[:, 0:width])
           .then_inc(SM["s_lorain"].h, SM["s_lorain"].adv(16)))
        wait("vector", SM["s_lorain"], SM["s_lorain"].v)
        on("vector", lambda e, dstt=dstt, width=width:
           e.tensor_copy(dstt[0:16, 0:width], lpre[0:16, 0:width])
           .then_inc(SM["v_pre"].h, SM["v_pre"].adv()))
        wait("sync", SM["v_pre"], SM["v_pre"].v)
    V_PRE_ALL = SM["v_pre"].v
    LORABF_ALL = SM["s_lorabf"].v

    set_phase(1)
    for i in range(8):
        if i > 0:
            wait("sync", SM["v_ln"], i)
        on("sync", lambda e, i=i: e.dma_start(out=xtok[:], in_=x_in[i * P:(i + 1) * P, :])
           .then_inc(SM["s_x"].h, SM["s_x"].adv(16)))
        wait("scalar", SM["s_x"], SM["s_x"].v)
        if i >= 2:
            wait("scalar", SM["t_tr"], 8 * (i - 1))
        on("scalar", lambda e, i=i: e.activation(xn_tok[i % 2][:], xtok[:], AF.Copy,
                                                 accum_out=stats[:, 0:1]))
        on("scalar", lambda e, i=i: e.activation(xn_tok[i % 2][:], xtok[:], AF.Square,
                                                 accum_out=stats[:, 1:2])
           .then_inc(SM["s_lns"].h, SM["s_lns"].adv()))
        wait("vector", SM["s_lns"], SM["s_lns"].v)
        on("vector", lambda e: e.tensor_scalar(stats[:, 2:3], stats[:, 0:1], 1.0 / C, None, ALU.mult))
        on("vector", lambda e: e.tensor_scalar(stats[:, 3:4], stats[:, 1:2], 1.0 / C, None, ALU.mult)
           .then_inc(SM["v_self"].h, SM["v_self"].adv()))
        wait("vector", SM["v_self"], SM["v_self"].v)
        on("vector", lambda e: e.tensor_tensor(stats[:, 4:5], stats[:, 2:3], stats[:, 2:3], ALU.mult)
           .then_inc(SM["v_self"].h, SM["v_self"].adv()))
        wait("vector", SM["v_self"], SM["v_self"].v)
        on("vector", lambda e: e.tensor_tensor(stats[:, 5:6], stats[:, 3:4], stats[:, 4:5], ALU.subtract)
           .then_inc(SM["v_self"].h, SM["v_self"].adv()))
        wait("scalar", SM["v_self"], SM["v_self"].v)
        on("scalar", lambda e: e.activation(stats[:, 6:7], stats[:, 5:6], AF.Sqrt,
                                            bias=consts[:, 0:1])
           .then_inc(SM["s_sqrt"].h, SM["s_sqrt"].adv()))
        wait("vector", SM["s_sqrt"], SM["s_sqrt"].v)
        on("vector", lambda e: e.reciprocal(stats[:, 7:8], stats[:, 6:7])
           .then_inc(SM["v_self"].h, SM["v_self"].adv()))
        wait("vector", SM["v_self"], SM["v_self"].v)
        if i >= 1:
            wait("vector", SM["t_tr"], SM["t_tr"].v - 8 if SM["t_tr"].v >= 8 else 0)
        on("vector", lambda e, i=i: e.tensor_scalar(xn_tok[i % 2][:], xtok[:],
                                                    stats[:, 2:3], stats[:, 7:8],
                                                    ALU.subtract, ALU.mult)
           .then_inc(SM["v_ln"].h, SM["v_ln"].adv()))
        wait("tensor", SM["v_ln"], SM["v_ln"].v)
        for b in range(8):
            half = b % 2
            if SM["t_tr"].v >= 2:
                wait("tensor", SM["v_tre"], SM["t_tr"].v - 1)
            def ttr(e, i=i, b=b, half=half):
                inst = None
                for q in range(4):
                    cc = b * 4 + q
                    inst = e.transpose(
                        ps_bb[half][:, q * 128:(q + 1) * 128],
                        xn_tok[i % 2][:, cc * 128:(cc + 1) * 128], ident[:])
                inst.then_inc(SM["t_tr"].h, SM["t_tr"].adv())
            on("tensor", ttr)
            wait("vector", SM["t_tr"], SM["t_tr"].v)
            def vtre(e, i=i, b=b, half=half):
                inst = None
                for q in range(4):
                    cc = b * 4 + q
                    inst = e.tensor_copy(
                        xn_f[:, cc * 1024 + i * 128:cc * 1024 + (i + 1) * 128],
                        ps_bb[half][:, q * 128:(q + 1) * 128])
                inst.then_inc(SM["v_tre"].h, SM["v_tre"].adv())
            on("vector", vtre)
    V_TRE_ALL = SM["v_tre"].v

    set_phase(2)
    dq_tiles = [("d", fc, 0, 512) for fc in range(NFC)]
    for wname in ("g", "u"):
        for cc in range(NCC):
            for (seg, W) in ((0, 512), (1, 512), (2, 384)):
                dq_tiles.append((wname, cc, seg, W))
    NDQ = len(dq_tiles)

    wait("tensor", SM["v_posb"], 1)
    dq_store_caps = []
    for ti, (wn, idx, seg, W) in enumerate(dq_tiles):
        par = ti % 2
        if wn == "d":
            codes_src = dcodesT[idx * P:(idx + 1) * P, 0:W]
            arep_src = damaxR[idx * P:(idx + 1) * P, 0:W]
            lt_off = 2 * C + idx * P
            lrhs = db_sl[0:16, 0:W]
        else:
            src_c = gcodesT if wn == "g" else ucodesT
            src_a = gamaxR if wn == "g" else uamaxR
            codes_src = src_c[idx * P:(idx + 1) * P, seg * 512:seg * 512 + W]
            arep_src = src_a[idx * P:(idx + 1) * P, seg * 512:seg * 512 + W]
            lt_off = (0 if wn == "g" else C) + idx * P
            lrhs = (gb_sl if wn == "g" else ub_sl)[0:16, seg * 512:seg * 512 + W]
        if ti >= 2:
            wait("sync", SM["v_dqw"], ti - 1)
        on("sync", lambda e, par=par, codes_src=codes_src, W=W:
           e.dma_start(out=dq_codes[par][:, 0:W], in_=codes_src)
           .then_inc(SM[f"s_cod{par}"].h, SM[f"s_cod{par}"].adv(16)))
        cod_sem, cod_cap = SM[f"s_cod{par}"], SM[f"s_cod{par}"].v
        on("sync", lambda e, par=par, arep_src=arep_src, W=W:
           e.dma_start(out=dq_arep[par][:, 0:W], in_=arep_src)
           .then_inc(SM[f"s_arep{par}"].h, SM[f"s_arep{par}"].adv(16)))
        arep_sem, arep_cap = SM[f"s_arep{par}"], SM[f"s_arep{par}"].v
        if ti == 0:
            wait("sync", SM["s_lorabf"], LORABF_ALL)
        if ti >= 2:
            wait("sync", SM["t_dql"], ti - 1)
        on("sync", lambda e, par=par, lt_off=lt_off:
           e.dma_start(out=dq_lt[par][0:16, :], in_=lorab_dram.ap()[:, lt_off:lt_off + P])
           .then_inc(SM[f"s_lt{par}"].h, SM[f"s_lt{par}"].adv(16)))
        lt_sem, lt_cap = SM[f"s_lt{par}"], SM[f"s_lt{par}"].v
        wait("vector", cod_sem, cod_cap)
        def vmask(e, par=par, W=W):
            e.tensor_copy(dq_cu16[:, 0:W], dq_codes[par][:, 0:W])
            e.tensor_scalar(dq_m0[:, 0:W], dq_cu16[:, 0:W], 1, None, ALU.bitwise_and)
            e.tensor_copy(dq_m0b[:, 0:W], dq_m0[:, 0:W])
            e.tensor_scalar(dq_m1[:, 0:W], dq_cu16[:, 0:W], 1, 1,
                            ALU.logical_shift_right, ALU.bitwise_and)
            e.tensor_scalar(dq_m2[:, 0:W], dq_cu16[:, 0:W], 2, 1,
                            ALU.logical_shift_right, ALU.bitwise_and)
            e.tensor_scalar(dq_m3[:, 0:W], dq_cu16[:, 0:W], 3, 1,
                            ALU.logical_shift_right, ALU.bitwise_and) \
                .then_inc(SM["v_dqm"].h, SM["v_dqm"].adv())
        on("vector", vmask)
        wait("scalar", SM["v_dqm"], SM["v_dqm"].v)
        if ti >= 1:
            wait("scalar", SM["v_dqw"], ti)
        def sblend(e, W=W):
            inst = None
            for k in range(8):
                lo = NF4[2 * k]
                hi = NF4[2 * k + 1]
                inst = e.activation(dq_b8[:, k * 512:k * 512 + W], dq_m0b[:, 0:W],
                                    AF.Copy, bias=float(lo), scale=float(hi - lo))
            inst.then_inc(SM["s_dqb"].h, SM["s_dqb"].adv())
        on("scalar", sblend)
        if ti == 0:
            wait("tensor", SM["v_pre"], V_PRE_ALL)
        wait("tensor", lt_sem, lt_cap)
        if ti >= 1:
            wait("tensor", SM["v_dqw"], ti)
        on("tensor", lambda e, par=par, W=W, lrhs=lrhs:
           e.matmul(ps_f[4][:, 0:W], dq_lt[par][0:16, :], lrhs,
                    start=True, stop=True).then_inc(SM["t_dql"].h, SM["t_dql"].adv()))
        wait("vector", SM["s_dqb"], SM["s_dqb"].v)
        wait("vector", arep_sem, arep_cap)
        wait("vector", SM["t_dql"], SM["t_dql"].v)
        if ti >= 2:
            wait("vector", *dq_store_caps[ti - 2])
        def vtree(e, par=par, W=W):
            for j in range(4):
                e.select(dq_t4[:, j * 512:j * 512 + W], dq_m1[:, 0:W],
                         dq_b8[:, (2 * j + 1) * 512:(2 * j + 1) * 512 + W],
                         dq_b8[:, (2 * j) * 512:(2 * j) * 512 + W])
            for k in range(2):
                e.select(dq_u2[:, k * 512:k * 512 + W], dq_m2[:, 0:W],
                         dq_t4[:, (2 * k + 1) * 512:(2 * k + 1) * 512 + W],
                         dq_t4[:, (2 * k) * 512:(2 * k) * 512 + W])
            e.select(dq_v1[:, 0:W], dq_m3[:, 0:W],
                     dq_u2[:, 512:512 + W], dq_u2[:, 0:W])
            e.tensor_tensor(dq_t4[:, 0:W], dq_v1[:, 0:W], dq_arep[par][:, 0:W], ALU.mult)
            e.tensor_tensor(dq_w[par][:, 0:W], dq_t4[:, 0:W],
                            ps_f[4][:, 0:W], ALU.add) \
                .then_inc(SM["v_dqw"].h, SM["v_dqw"].adv())
        on("vector", vtree)
        wait("scalar", SM["v_dqw"], SM["v_dqw"].v)
        def qstore(e, wn=wn, idx=idx, seg=seg, W=W, par=par):
            if wn == "d":
                e.dma_start(out=dcon.ap()[idx * P:(idx + 1) * P, :],
                            in_=dq_w[par][:, 0:W]).then_inc(SM[f"s_dst{par}"].h, SM[f"s_dst{par}"].adv(16))
            else:
                con = gcon if wn == "g" else ucon
                nb = W // 128
                fb0 = seg * 4
                e.dma_start(
                    out=con.ap()[idx * FSP + fb0 * P:idx * FSP + (fb0 + nb) * P, :]
                    .rearrange("(b p) f -> p b f", p=P),
                    in_=dq_w[par][:, 0:W].rearrange("p (b f) -> p b f", f=128)
                ).then_inc(SM[f"s_dst{par}"].h, SM[f"s_dst{par}"].adv(16))
        on("scalar", qstore)
        dq_store_caps.append((SM[f"s_dst{par}"], SM[f"s_dst{par}"].v))
    V_DQW_ALL = SM["v_dqw"].v
    DQ_ST_ALL = [(SM["s_dst0"], SM["s_dst0"].v), (SM["s_dst1"], SM["s_dst1"].v)]

    set_phase(3)
    kv_store_caps = {}
    strip_free = {"kv": [], "qo": []}

    def emit_proj(wname, strip_bufs, kind):
        sfree = strip_free["qo" if kind == "q" else "kv"]
        for dc in range(NCC):
            par = dc % 2
            if len(sfree) >= 2:
                wait("gpsimd", SM["v_pje"], sfree[-2])
            def gload(e, wname=wname, dc=dc, par=par, strip_bufs=strip_bufs):
                e.dma_start(
                    out=strip_bufs[par].rearrange("p (a d) -> p a d", d=128),
                    in_=wT[wname][:, dc * 128:(dc + 1) * 128]
                    .rearrange("(a p) d -> p a d", p=P)
                ).then_inc(SM[f"g_st{par}"].h, SM[f"g_st{par}"].adv(16))
            on("gpsimd", gload)
            wait("tensor", SM[f"g_st{par}"], SM[f"g_st{par}"].v)
            if kind == "k" and dc == 0:
                wait("tensor", SM["v_tre"], V_TRE_ALL)
            for th in range(2):
                bank = (dc % 2) * 2 + th
                grp = SM["t_pj"].v + 1
                if grp > 4:
                    wait("tensor", SM["v_pje"], grp - 4)
                def tmm(e, dc=dc, th=th, bank=bank, par=par, strip_bufs=strip_bufs):
                    inst = None
                    for cc in range(NCC):
                        inst = e.matmul(
                            ps_f[bank][:],
                            strip_bufs[par][:, cc * 128:(cc + 1) * 128],
                            xn_f[:, cc * 1024 + th * 512:cc * 1024 + (th + 1) * 512],
                            start=(cc == 0), stop=(cc == NCC - 1))
                    inst.then_inc(SM["t_pj"].h, SM["t_pj"].adv())
                on("tensor", tmm)
                wait("vector", SM["t_pj"], SM["t_pj"].v)
                if kind in ("k", "q"):
                    j = dc * 2 + th
                    if kind == "k" and j >= 2:
                        wait("vector", *kv_store_caps[("k", j - 2)])
                    def vrope_evac(e, dc=dc, th=th, bank=bank, kind=kind):
                        cs = sincos[:, TL + th * 512:TL + (th + 1) * 512]
                        sn = sincos[:, th * 512:(th + 1) * 512]
                        dst = (kev[(dc * 2 + th) % 2][:] if kind == "k"
                               else qT[:, dc * 1024 + th * 512:dc * 1024 + (th + 1) * 512])
                        p = ps_f[bank]
                        e.tensor_tensor(rt1[0:64, :], p[0:64, :], cs[0:64, :], ALU.mult)
                        e.scalar_tensor_tensor(rt2[0:64, :], p[64:128, :], -1.0,
                                               sn[0:64, :], ALU.mult, ALU.mult)
                        e.tensor_tensor(dst[0:64, :], rt1[0:64, :], rt2[0:64, :], ALU.add)
                        e.tensor_tensor(rt1[64:128, :], p[64:128, :], cs[64:128, :], ALU.mult)
                        e.tensor_tensor(rt2[64:128, :], p[0:64, :], sn[64:128, :], ALU.mult)
                        e.tensor_tensor(dst[64:128, :], rt1[64:128, :],
                                        rt2[64:128, :], ALU.add) \
                            .then_inc(SM["v_pje"].h, SM["v_pje"].adv())
                    on("vector", vrope_evac)
                    if th == 1:
                        sfree.append(SM["v_pje"].v)
                    if kind == "k":
                        wait("scalar", SM["v_pje"], SM["v_pje"].v)
                        jp = (dc * 2 + th) % 2
                        on("scalar", lambda e, dc=dc, th=th, jp=jp:
                           e.dma_start(out=kcon.ap()[dc * 128:(dc + 1) * 128,
                                                     th * 512:(th + 1) * 512],
                                       in_=kev[(dc * 2 + th) % 2][:])
                           .then_inc(SM[f"s_kst{jp}"].h, SM[f"s_kst{jp}"].adv(16)))
                        kv_store_caps[("k", dc * 2 + th)] = (SM[f"s_kst{jp}"], SM[f"s_kst{jp}"].v)
                else:
                    def vv_evac(e, th=th, bank=bank):
                        e.tensor_copy(vev[th][:], ps_f[bank][:]) \
                            .then_inc(SM["v_pje"].h, SM["v_pje"].adv())
                    on("vector", vv_evac)
                    if th == 1:
                        sfree.append(SM["v_pje"].v)
                    wait("tensor", SM["v_pje"], SM["v_pje"].v)
                    if SM["t_vtr"].v >= 2:
                        wait("tensor", SM["v_vte"], SM["t_vtr"].v - 1)
                    def tvt(e, th=th):
                        inst = None
                        for q in range(4):
                            inst = e.transpose(
                                ps_bb[th][:, q * 128:(q + 1) * 128],
                                vev[th][:, q * 128:(q + 1) * 128], ident[:])
                        inst.then_inc(SM["t_vtr"].h, SM["t_vtr"].adv())
                    on("tensor", tvt)
                    wait("vector", SM["t_vtr"], SM["t_vtr"].v)
                    j = dc * 2 + th
                    if j >= 2:
                        wait("vector", *kv_store_caps[("v", j - 2)])
                    on("vector", lambda e, th=th:
                       e.tensor_copy(vtr[th][:], ps_bb[th][:, 0:512])
                       .then_inc(SM["v_vte"].h, SM["v_vte"].adv()))
                    wait("scalar", SM["v_vte"], SM["v_vte"].v)
                    jp = j % 2
                    on("scalar", lambda e, dc=dc, th=th, jp=jp:
                       e.dma_start(out=vcon.ap()[dc * 1024 + th * 512:
                                                 dc * 1024 + (th + 1) * 512, :]
                                   .rearrange("(q p) d -> p q d", p=P),
                                   in_=vtr[th][:].rearrange("p (q d) -> p q d", d=128))
                       .then_inc(SM[f"s_vst{jp}"].h, SM[f"s_vst{jp}"].adv(16)))
                    kv_store_caps[("v", j)] = (SM[f"s_vst{jp}"], SM[f"s_vst{jp}"].v)

    set_phase(31)
    emit_proj("k", strip_kv, "k")
    set_phase(32)
    emit_proj("v", strip_kv, "v")
    set_phase(33)
    for s in ["s_kst0", "s_kst1", "s_vst0", "s_vst1"]:
        wait("gpsimd", SM[s], SM[s].v)
    on("gpsimd", lambda e: e.collective_compute(
        "AllGather", ALU.bypass, replica_groups=[[0, 1], [2, 3], [4, 5], [6, 7]],
        ins=[kcon.ap()], outs=[kg.ap()]).then_inc(SM["g_cckv"].h, SM["g_cckv"].adv()))
    on("gpsimd", lambda e: e.collective_compute(
        "AllGather", ALU.bypass, replica_groups=[[0, 1], [2, 3], [4, 5], [6, 7]],
        ins=[vcon.ap()], outs=[vg.ap()]).then_inc(SM["g_cckv"].h, SM["g_cckv"].adv()))
    set_phase(34)
    emit_proj("q", strip_qo, "q")
    T_PJ_ALL = SM["t_pj"].v
    V_PJE_ALL = SM["v_pje"].v
    set_phase(35)
    for s, v in DQ_ST_ALL:
        wait("gpsimd", s, v)
    on("gpsimd", lambda e: e.collective_compute(
        "AllGather", ALU.bypass, replica_groups=[list(range(8))],
        ins=[gcon.ap()], outs=[gg.ap()]).then_inc(SM["g_ccw"].h, SM["g_ccw"].adv()))
    on("gpsimd", lambda e: e.collective_compute(
        "AllGather", ALU.bypass, replica_groups=[list(range(8))],
        ins=[ucon.ap()], outs=[ug.ap()]).then_inc(SM["g_ccw"].h, SM["g_ccw"].adv()))
    on("gpsimd", lambda e: e.collective_compute(
        "AllGather", ALU.bypass, replica_groups=[list(range(8))],
        ins=[dcon.ap()], outs=[dg.ap()]).then_inc(SM["g_ccw"].h, SM["g_ccw"].adv()))

    set_phase(36)
    wait("scalar", SM["v_pje"], V_PJE_ALL)
    on("scalar", lambda e: e.dma_start(out=xnf_dram.ap().rearrange("(a p) t -> p a t", p=P),
                                       in_=xn_f[:].rearrange("p (a t) -> p a t", t=TL))
       .then_inc(SM["s_xnf"].h, SM["s_xnf"].adv(16)))

    set_phase(40)
    wait("vector", SM["t_pj"], T_PJ_ALL)
    BIN_IDX = {}
    bi = 0
    for J in range(2):
        for kk in range(4 * J, KCNT[J]):
            BIN_IDX[(J, kk)] = bi
            bi += 1

    def vbins(e):
        inst = None
        for J in range(2):
            for kk in range(4 * J, KCNT[J]):
                inst = e.tensor_scalar(
                    at_bins[:, BIN_IDX[(J, kk)] * 512:(BIN_IDX[(J, kk)] + 1) * 512],
                    posq[:, J * 512:(J + 1) * 512],
                    posk[:, kk:kk + 1], None, ALU.is_ge)
        inst.then_inc(SM["v_bin"].h, SM["v_bin"].adv())
    on("vector", vbins)

    slot = 0
    for h in range(H):
        hp = h % 2
        if h == 0:
            wait("sync", SM["g_cckv"], 2)
            wait("sync", SM["v_dqw"], NDQ)
            wait("sync", SM["v_pje"], V_PJE_ALL)
        if h >= 2:
            wait("sync", SM["v_oev"], 2 * (h - 1))
        def kvload(e, h=h, hp=hp):
            for rr in range(2):
                e.dma_start(out=at_kh[hp][:, rr * 1024:(rr + 1) * 1024],
                            in_=kg.ap()[rr * 4096 + h * 128:rr * 4096 + (h + 1) * 128, :]) \
                    .then_inc(SM[f"s_kv{hp}"].h, SM[f"s_kv{hp}"].adv(16))
                e.dma_start(
                    out=at_vh[hp][:, rr * 1024:(rr + 1) * 1024]
                    .rearrange("p (a d) -> p a d", d=128),
                    in_=vg.ap()[rr * 32768 + h * 1024:rr * 32768 + (h + 1) * 1024, :]
                    .rearrange("(a p) d -> p a d", p=P)) \
                    .then_inc(SM[f"s_kv{hp}"].h, SM[f"s_kv{hp}"].adv(16))
        on("sync", kvload)
        kv_sem, kv_cap = SM[f"s_kv{hp}"], SM[f"s_kv{hp}"].v
        for J in range(2):
            hj = 2 * h + J
            last_kk = KCNT[J] - 1
            for kk in range(KCNT[J]):
                sp = slot % 2
                if J == 0 and kk == 0:
                    wait("tensor", kv_sem, kv_cap)
                    wait("tensor", SM["v_pje"],
                         V_PJE_ALL if h == 0 else 128 + 2 * (h + 1))
                if slot >= 2:
                    wait("tensor", SM["s_px"], slot - 1)
                on("tensor", lambda e, h=h, J=J, kk=kk, sp=sp, hp=hp:
                   e.matmul(ps_f[sp][:], at_kh[hp][:, kk * 128:(kk + 1) * 128],
                            qT[:, h * 1024 + J * 512:h * 1024 + (J + 1) * 512],
                            start=True, stop=True)
                   .then_inc(SM["t_s"].h, SM["t_s"].adv()))
                wait("scalar", SM["t_s"], SM["t_s"].v)
                if slot >= 2:
                    wait("scalar", SM["v_pm"], slot - 1)
                    wait("scalar", SM["t_pv"], slot - 1)
                on("scalar", lambda e, sp=sp:
                   e.activation(at_pex[sp][:], ps_f[sp][:], AF.Exp, scale=SM_SCALE)
                   .then_inc(SM["s_px"].h, SM["s_px"].adv()))
                wait("vector", SM["s_px"], SM["s_px"].v)
                if slot == 0:
                    wait("vector", SM["v_bin"], 1)
                if slot >= 2:
                    wait("vector", SM["t_pv"], slot - 1)
                masked = kk >= 4 * J
                zpar = kk % 2
                if kk == 1 and hj >= 1:
                    wait("vector", SM["t_zc"], SM["t_zc"].v - 1 if SM["t_zc"].v >= 1 else 0)
                def vpm(e, J=J, kk=kk, sp=sp, masked=masked, zpar=zpar):
                    src = at_pex[sp]
                    if masked:
                        e.tensor_tensor(at_pm[sp][:], at_pex[sp][:],
                                        at_bins[:, BIN_IDX[(J, kk)] * 512:
                                                (BIN_IDX[(J, kk)] + 1) * 512], ALU.mult)
                        src = at_pm[sp]
                    zdst = zacc[:, (J * 2 + zpar) * 512:(J * 2 + zpar + 1) * 512]
                    if kk == 0:
                        e.tensor_copy(zdst, src[:]).then_inc(SM["v_pm"].h, SM["v_pm"].adv())
                    else:
                        zsrc = zacc[:, (J * 2 + 1 - zpar) * 512:(J * 2 + 2 - zpar) * 512]
                        e.tensor_tensor(zdst, zsrc, src[:], ALU.add) \
                            .then_inc(SM["v_pm"].h, SM["v_pm"].adv())
                on("vector", vpm)
                wait("tensor", SM["v_pm"], SM["v_pm"].v)
                if kk == 0 and hj >= 2:
                    wait("tensor", SM["v_oev"], hj - 1)
                on("tensor", lambda e, J=J, kk=kk, sp=sp, hp=hp, masked=masked,
                   last=(kk == last_kk):
                   e.matmul(ps_f[2 + J][:], at_vh[hp][:, kk * 128:(kk + 1) * 128],
                            (at_pm[sp] if masked else at_pex[sp])[:],
                            start=(kk == 0), stop=last)
                   .then_inc(SM["t_pv"].h, SM["t_pv"].adv()))
                slot += 1
            zlast = (KCNT[J] - 1) % 2
            if hj >= 1:
                wait("tensor", SM["v_rz"], hj)
            on("tensor", lambda e, J=J, zlast=zlast:
               e.matmul(ps_z[:], ones_col[:],
                        zacc[:, (J * 2 + zlast) * 512:(J * 2 + zlast + 1) * 512],
                        start=True, stop=True).then_inc(SM["t_zc"].h, SM["t_zc"].adv()))
            wait("vector", SM["t_zc"], SM["t_zc"].v)
            if hj >= 1:
                wait("vector", SM["t_rzb"], SM["t_rzb"].v)
            on("vector", lambda e: e.reciprocal(rz, ps_z[:])
               .then_inc(SM["v_rz"].h, SM["v_rz"].adv()))
            wait("tensor", SM["v_rz"], SM["v_rz"].v)
            if hj >= 1:
                wait("tensor", SM["v_oev"], hj)
            on("tensor", lambda e: e.matmul(ps_f[4][:], ones_row[:], rz[0:1, :],
                                            start=True, stop=True)
               .then_inc(SM["t_rzb"].h, SM["t_rzb"].adv()))
            wait("vector", SM["t_rzb"], SM["t_rzb"].v)
            if hj == 0:
                wait("vector", SM["s_xnf"], 16)
            def voev(e, h=h, J=J):
                e.tensor_copy(rzrep, ps_f[4][:])
                e.tensor_tensor(outT[:, h * 1024 + J * 512:h * 1024 + (J + 1) * 512],
                                ps_f[2 + J][:], rzrep, ALU.mult) \
                    .then_inc(SM["v_oev"].h, SM["v_oev"].adv())
            on("vector", voev)
    T_S_ALL = SM["t_s"].v
    V_OEV_ALL = SM["v_oev"].v
    T_PV_ALL = SM["t_pv"].v

    set_phase(50)
    op_store_caps = []
    for ds in range(8):
        par = ds % 2
        if ds == 0:
            wait("gpsimd", SM["v_oev"], V_OEV_ALL)
        if ds >= 2:
            wait("gpsimd", SM["t_op"], (ds - 1) * 8)
        def gwo(e, ds=ds, par=par):
            e.dma_start(out=wo_buf[par].rearrange("p (a w) -> p a w", w=512),
                        in_=wT["o"][:, ds * 512:(ds + 1) * 512]
                        .rearrange("(a p) w -> p a w", p=P)) \
                .then_inc(SM[f"g_wo{par}"].h, SM[f"g_wo{par}"].adv(16))
        on("gpsimd", gwo)
        wo_sem, wo_cap = SM[f"g_wo{par}"], SM[f"g_wo{par}"].v
        for tc in range(8):
            g = ds * 8 + tc
            bank = g % 4
            wait("tensor", wo_sem, wo_cap)
            if g == 0:
                wait("tensor", SM["v_oev"], V_OEV_ALL)
            if g >= 4:
                wait("tensor", SM["v_ope"], g - 3)
            def top(e, tc=tc, par=par, bank=bank):
                inst = None
                for cc in range(NCC):
                    inst = e.matmul(ps_f[bank][:],
                                    outT[:, cc * 1024 + tc * 128:cc * 1024 + (tc + 1) * 128],
                                    wo_buf[par][:, cc * 512:(cc + 1) * 512],
                                    start=(cc == 0), stop=(cc == NCC - 1))
                inst.then_inc(SM["t_op"].h, SM["t_op"].adv())
            on("tensor", top)
            wait("vector", SM["t_op"], SM["t_op"].v)
            spar = g % 2
            if g >= 2:
                wait("vector", *op_store_caps[g - 2])
            on("vector", lambda e, spar=spar, bank=bank:
               e.tensor_copy(op_stage[spar], ps_f[bank][:])
               .then_inc(SM["v_ope"].h, SM["v_ope"].adv()))
            wait("scalar", SM["v_ope"], SM["v_ope"].v)
            on("scalar", lambda e, ds=ds, tc=tc, spar=spar:
               e.dma_start(out=attn_dram.ap()[tc * 128:(tc + 1) * 128,
                                              ds * 512:(ds + 1) * 512],
                           in_=op_stage[spar])
               .then_inc(SM[f"s_opst{spar}"].h, SM[f"s_opst{spar}"].adv(16)))
            op_store_caps.append((SM[f"s_opst{spar}"], SM[f"s_opst{spar}"].v))
    T_OP_ALL = SM["t_op"].v
    OP_ST_ALL = [(SM["s_opst0"], SM["s_opst0"].v), (SM["s_opst1"], SM["s_opst1"].v)]

    set_phase(60)
    wait("sync", SM["t_s"], T_S_ALL)
    on("sync", lambda e: e.dma_start(out=xn_f2[:].rearrange("p (a t) -> p a t", t=TL),
                                     in_=xnf_dram.ap().rearrange("(a p) t -> p a t", p=P))
       .then_inc(SM["s_xnfl"].h, SM["s_xnfl"].adv(16)))

    gg_v = gg.ap().rearrange("(r a b p) f -> r a b p f", r=8, a=NCC, b=11, p=P)
    ug_v = ug.ap().rearrange("(r a b p) f -> r a b p f", r=8, a=NCC, b=11, p=P)
    x3_store_caps = []
    for fc in range(NFC):
        r, fb = fc // 11, fc % 11
        par = fc % 2
        if fc == 0:
            wait("sync", SM["g_ccw"], 3)
            wait("sync", SM["t_op"], T_OP_ALL)
        if fc >= 2:
            wait("sync", SM["t_y"], (fc - 1) * 4)
        def fgl(e, r=r, fb=fb, par=par):
            e.dma_start(out=fg_w[par].rearrange("p (a f) -> p a f", f=128),
                        in_=gg_v[r, :, fb].rearrange("a p f -> p a f")) \
                .then_inc(SM[f"s_fg{par}"].h, SM[f"s_fg{par}"].adv(16))
            e.dma_start(out=fg_w[2 + par].rearrange("p (a f) -> p a f", f=128),
                        in_=ug_v[r, :, fb].rearrange("a p f -> p a f")) \
                .then_inc(SM[f"s_fg{par}"].h, SM[f"s_fg{par}"].adv(16))
        on("sync", fgl)
        wait("tensor", SM[f"s_fg{par}"], SM[f"s_fg{par}"].v)
        if fc == 0:
            wait("tensor", SM["s_xnfl"], 16)
        if fc >= 1:
            wait("tensor", SM["v_x3"], (fc - 1) * 2 + 2)
            wait("tensor", SM["s_y2c"], (fc - 1) * 2 + 2)
        def tgu(e, par=par):
            inst = None
            for tt in range(2):
                for cc in range(NCC):
                    e.matmul(ps_f[0 + tt][:], fg_w[par][:, cc * 128:(cc + 1) * 128],
                             xn_f2[:, cc * 1024 + tt * 512:cc * 1024 + (tt + 1) * 512],
                             start=(cc == 0), stop=(cc == NCC - 1))
                for cc in range(NCC):
                    inst = e.matmul(ps_f[2 + tt][:], fg_w[2 + par][:, cc * 128:(cc + 1) * 128],
                                    xn_f2[:, cc * 1024 + tt * 512:cc * 1024 + (tt + 1) * 512],
                                    start=(cc == 0), stop=(cc == NCC - 1))
            inst.then_inc(SM["t_y"].h, SM["t_y"].adv(4))
        on("tensor", tgu)
        for tt in range(2):
            wait("scalar", SM["t_y"], SM["t_y"].v)
            if fc >= 1:
                wait("scalar", SM["v_x3"], SM["v_x3"].v - 1)
            on("scalar", lambda e, tt=tt: e.activation(fy2[tt][:], ps_f[2 + tt][:], AF.Copy)
               .then_inc(SM["s_y2c"].h, SM["s_y2c"].adv()))
            wait("vector", SM["s_y2c"], SM["s_y2c"].v)
            if len(x3_store_caps) >= 2:
                wait("vector", *x3_store_caps[-2])
            on("vector", lambda e, tt=tt:
               e.scalar_tensor_tensor(fx3[tt][:], ps_f[0 + tt][:], 0.0, fy2[tt][:],
                                      ALU.max, ALU.mult)
               .then_inc(SM["v_x3"].h, SM["v_x3"].adv()))
            wait("scalar", SM["v_x3"], SM["v_x3"].v)
            on("scalar", lambda e, fc=fc, tt=tt:
               e.dma_start(out=x3_dram.ap()[fc * P:(fc + 1) * P, tt * 512:(tt + 1) * 512],
                           in_=fx3[tt][:]).then_inc(SM[f"s_x3st{tt}"].h, SM[f"s_x3st{tt}"].adv(16)))
            x3_store_caps.append((SM[f"s_x3st{tt}"], SM[f"s_x3st{tt}"].v))
    X3_ST_ALL = [(SM["s_x3st0"], SM["s_x3st0"].v), (SM["s_x3st1"], SM["s_x3st1"].v)]

    set_phase(70)
    dg_v = dg.ap()
    nload = 0
    out_store_caps = []
    fd_consume = []
    for o in range(8):
        for tch in range(2):
            oi = o * 2 + tch
            gpar = 0
            for fc in range(NFC):
                if fc % 8 == 0:
                    gpar = nload % 2
                    if oi == 0 and fc == 0:
                        for s, v in X3_ST_ALL:
                            wait("sync", s, v)
                    if nload >= 2:
                        wait("sync", SM["t_d"], fd_consume[nload - 2])
                    n = min(8, NFC - fc)
                    def fdl(e, o=o, fc=fc, gpar=gpar, n=n):
                        e.dma_start(
                            out=fd_w[gpar][:, 0:n * 512].rearrange("p (a w) -> p a w", w=512),
                            in_=dg_v[o * DFFP + fc * P:o * DFFP + (fc + n) * P, :]
                            .rearrange("(a p) w -> p a w", p=P)) \
                            .then_inc(SM[f"s_fd{gpar}"].h, SM[f"s_fd{gpar}"].adv(16))
                    on("sync", fdl)
                    fd_sem, fd_cap = SM[f"s_fd{gpar}"], SM[f"s_fd{gpar}"].v
                    fd_consume.append(oi * NFC + min(fc + 8, NFC))
                    nload += 1
                xpar = fc % 2
                if SM["t_d"].v >= 1 and oi * NFC + fc >= 2:
                    wait("sync", SM["t_d"], SM["t_d"].v - 1)
                def fxl(e, fc=fc, tch=tch, xpar=xpar):
                    e.dma_start(out=fd_x3[xpar][:],
                                in_=x3_dram.ap()[fc * P:(fc + 1) * P,
                                                 tch * 512:(tch + 1) * 512]) \
                        .then_inc(SM[f"s_xr{xpar}"].h, SM[f"s_xr{xpar}"].adv(16))
                on("sync", fxl)
                wait("tensor", SM[f"s_xr{xpar}"], SM[f"s_xr{xpar}"].v)
                if fc % 8 == 0:
                    wait("tensor", fd_sem, fd_cap)
                if fc == 0 and oi >= 1:
                    wait("tensor", SM["v_de"], (oi - 1) * 4 + 4)
                def tdm(e, fc=fc, gpar=gpar, xpar=xpar):
                    woff = (fc % 8) * 512
                    inst = None
                    for tq in range(4):
                        inst = e.matmul(ps_f[tq][:],
                                        fd_x3[xpar][:, tq * 128:(tq + 1) * 128],
                                        fd_w[gpar][:, woff:woff + 512],
                                        start=(fc == 0), stop=(fc == NFC - 1))
                    inst.then_inc(SM["t_d"].h, SM["t_d"].adv())
                on("tensor", tdm)
            T_D_CUR = SM["t_d"].v
            for tq in range(4):
                tglob = tch * 4 + tq
                epar = tq % 2
                ei = oi * 4 + tq
                if oi == 0 and tq == 0:
                    for s, v in OP_ST_ALL:
                        wait("sync", s, v)
                if ei >= 2:
                    wait("sync", SM["v_de"], ei - 1)
                def aol(e, o=o, tglob=tglob, epar=epar):
                    e.dma_start(out=fo_at[epar][:],
                                in_=attn_dram.ap()[tglob * 128:(tglob + 1) * 128,
                                                   o * 512:(o + 1) * 512]) \
                        .then_inc(SM[f"s_ao{epar}"].h, SM[f"s_ao{epar}"].adv(16))
                    e.dma_start(out=fo_x[epar][:],
                                in_=x_in[tglob * 128:(tglob + 1) * 128,
                                         o * 512:(o + 1) * 512]) \
                        .then_inc(SM[f"s_ao{epar}"].h, SM[f"s_ao{epar}"].adv(16))
                on("sync", aol)
                ao_sem, ao_cap = SM[f"s_ao{epar}"], SM[f"s_ao{epar}"].v
                wait("vector", SM["t_d"], T_D_CUR)
                wait("vector", ao_sem, ao_cap)
                if ei >= 2:
                    wait("vector", *out_store_caps[ei - 2])
                def vde(e, tq=tq, epar=epar):
                    e.tensor_tensor(fo_a[epar][:], ps_f[tq][:], fo_at[epar][:], ALU.add)
                    e.tensor_tensor(fo_b[epar][:], fo_a[epar][:], fo_x[epar][:], ALU.add) \
                        .then_inc(SM["v_de"].h, SM["v_de"].adv())
                on("vector", vde)
                wait("scalar", SM["v_de"], SM["v_de"].v)
                on("scalar", lambda e, o=o, tglob=tglob, epar=epar:
                   e.dma_start(out=out_ext[tglob * 128:(tglob + 1) * 128,
                                           o * 512:(o + 1) * 512],
                               in_=fo_b[epar][:])
                   .then_inc(SM[f"s_outst{epar}"].h, SM[f"s_outst{epar}"].adv(16)))
                out_store_caps.append((SM[f"s_outst{epar}"], SM[f"s_outst{epar}"].v))
    wait("scalar", SM["s_outst0"], SM["s_outst0"].v)
    wait("scalar", SM["s_outst1"], SM["s_outst1"].v)

    Ctr.TRACE = None
    with stk:
        with nc.Block() as block:
            def replay(eng, name):
                for e, fn, ph in prog:
                    if e == name and ph <= phase_limit:
                        fn(eng)

            @block.sync
            def _(eng):
                replay(eng, "sync")

            @block.gpsimd
            def _(eng):
                replay(eng, "gpsimd")

            @block.vector
            def _(eng):
                replay(eng, "vector")

            @block.scalar
            def _(eng):
                replay(eng, "scalar")

            @block.tensor
            def _(eng):
                replay(eng, "tensor")

    return nc


_NC_CACHE = None


def _get_nc():
    global _NC_CACHE
    if _NC_CACHE is None:
        _NC_CACHE = build_nc()
    return _NC_CACHE


def kernel(**inputs):
    x = np.ascontiguousarray(np.asarray(inputs["L0_attn_x"], dtype=np.float32))
    B, T, _ = x.shape
    wqT = np.ascontiguousarray(np.asarray(inputs["L0_attn_wq"], np.float32).T)
    wkT = np.ascontiguousarray(np.asarray(inputs["L0_attn_wk"], np.float32).T)
    wvT = np.ascontiguousarray(np.asarray(inputs["L0_attn_wv"], np.float32).T)
    woT = np.ascontiguousarray(np.asarray(inputs["L0_attn_wo"], np.float32).T)

    g_codes = np.asarray(inputs["L0_ffn_g_codes"], np.int32)
    u_codes = np.asarray(inputs["L0_ffn_u_codes"], np.int32)
    d_codes = np.asarray(inputs["L0_ffn_d_codes"], np.int32)
    g_amax = np.asarray(inputs["L0_ffn_g_absmax"], np.float32)
    u_amax = np.asarray(inputs["L0_ffn_u_absmax"], np.float32)
    d_amax = np.asarray(inputs["L0_ffn_d_absmax"], np.float32)
    gaT = np.ascontiguousarray(np.asarray(inputs["L0_ffn_g_lora_a"], np.float32).T)
    uaT = np.ascontiguousarray(np.asarray(inputs["L0_ffn_u_lora_a"], np.float32).T)
    daT = np.zeros((16, DFFP), np.float32)
    daT[:, :DFF] = np.asarray(inputs["L0_ffn_d_lora_a"], np.float32).T
    gb = np.asarray(inputs["L0_ffn_g_lora_b"], np.float32)
    ub = np.asarray(inputs["L0_ffn_u_lora_b"], np.float32)
    db = np.asarray(inputs["L0_ffn_d_lora_b"], np.float32)

    in_maps = []
    for r in range(8):
        b, half = r // 2, r % 2
        xs = np.ascontiguousarray(x[b, half * TL:(half + 1) * TL, :])
        pos = np.ascontiguousarray(
            np.arange(half * TL, (half + 1) * TL, dtype=np.float32)[None, :])
        f0 = r * FSP
        f1 = min(f0 + FSP, DFF)
        nsl = max(0, f1 - f0)
        gsl = np.zeros((C, FSP), np.int32)
        usl = np.zeros((C, FSP), np.int32)
        gam = np.zeros((C, FSP), np.float32)
        uam = np.zeros((C, FSP), np.float32)
        gbs = np.zeros((16, FSP), np.float32)
        ubs = np.zeros((16, FSP), np.float32)
        if nsl > 0:
            gsl[:, :nsl] = g_codes[f0:f1, :].T
            usl[:, :nsl] = u_codes[f0:f1, :].T
            gam[:, :nsl] = np.repeat(g_amax[f0:f1, :].T, 64, axis=0)
            uam[:, :nsl] = np.repeat(u_amax[f0:f1, :].T, 64, axis=0)
            gbs[:, :nsl] = gb[:, f0:f1]
            ubs[:, :nsl] = ub[:, f0:f1]
        o0 = r * OSL
        dsl = np.zeros((DFFP, OSL), np.int32)
        dsl[:DFF, :] = d_codes[o0:o0 + OSL, :].T
        dam = np.zeros((DFFP, OSL), np.float32)
        dam[:DFF, :] = np.repeat(d_amax[o0:o0 + OSL, :].T, 64, axis=0)
        dbs = np.ascontiguousarray(db[:, o0:o0 + OSL])
        in_maps.append(dict(
            x_in=xs, pos_in=pos, wqT=wqT, wkT=wkT, wvT=wvT, woT=woT,
            gcodesT=np.ascontiguousarray(gsl), ucodesT=np.ascontiguousarray(usl),
            dcodesT=dsl, gamaxR=gam, uamaxR=uam, damaxR=dam,
            gaT_in=gaT, uaT_in=uaT, daT_in=np.ascontiguousarray(daT),
            gb_in=gbs, ub_in=ubs, db_in=dbs))

    nc = _get_nc()
    res = run_bass_kernel_spmd(nc, in_maps, core_ids=list(range(8)))
    out = np.empty((B, T, C), np.float32)
    for r in range(8):
        b, half = r // 2, r % 2
        out[b, half * TL:(half + 1) * TL, :] = res.results[r]["out"]
    return out
